# version 06
# baseline (speedup 1.0000x reference)
import numpy as np
from contextlib import ExitStack
import concourse.bass as bass
import concourse.mybir as mybir

F32 = mybir.dt.float32
I32 = mybir.dt.int32
BF16 = mybir.dt.bfloat16
AF = mybir.ActivationFunctionType
ALU = mybir.AluOpType
AX = mybir.AxisListType

D = 2048
TI = 1408
NU = 18
NI = 11
ND = 16
TT = 2048
TH = 1024
NJ = 8
EPS = 1e-5
BIG = 1e30
NEG_BIG = -1e30
RING13 = 3

ENGS = ("tensor", "vector", "scalar", "sync", "gpsimd")


class Mk:
    def __init__(self, nc):
        self.nc = nc
        self.prog = {e: [] for e in ENGS}
        self.cnt = {}
        self.sem = {}
        self.bank_free = {}
        self.bank_ready = {}
        self.emitted_wait = {}
        self.w13_slot_free = [0] * RING13
        self.w2_slot_free = [0] * 3
        self.sil_free = [[0, 0], [0, 0]]
        self.w13_seq = 0
        self.w13_seq2 = 0
        from collections import deque
        self.w13_jobs = deque()
        self.w2_jobs = deque()
        self.w13_loaded = {}
        self.w2_loaded = {}
        self.w2_emit_seq = 0
        self.w2_cons_seq = 0
        self.call_seq = 0

    def new_sem(self, name, es):
        self.sem[name] = es.enter_context(self.nc.semaphore(name))
        self.cnt[name] = 0

    def emit(self, eng, fn):
        self.prog[eng].append(fn)

    def wait(self, eng, name, val):
        if val <= 0:
            return
        key = (eng, name)
        if val <= self.emitted_wait.get(key, -1):
            return
        self.emitted_wait[key] = val
        s = self.sem[name]
        self.emit(eng, lambda e, s=s, v=val: e.wait_ge(s, v))

    def wait_bank_free(self, eng, bank):
        if bank in self.bank_free:
            nm, v = self.bank_free[bank]
            self.wait(eng, nm, v)

    def wait_bank_ready(self, eng, bank):
        nm, v = self.bank_ready[bank]
        self.wait(eng, nm, v)

    def op(self, eng, name, n, fn):
        self.cnt[name] += n
        v = self.cnt[name]
        s = self.sem[name]
        self.emit(eng, lambda e, fn=fn, s=s, n=n: fn(e).then_inc(s, n))
        return v

    def raw(self, eng, fn):
        self.emit(eng, fn)

    def wait_stores(self, eng):
        for nm in ("xsa", "xsb", "xsc", "xsd"):
            self.wait(eng, nm, self.cnt[nm])

    def chain(self, eng, name, fn):
        self.wait(eng, name, self.cnt[name])
        return self.op(eng, name, 1, fn)

    def run_block(self):
        nc = self.nc
        with nc.Block() as block:
            @block.tensor
            def _(t):
                for f in self.prog["tensor"]:
                    f(t)

            @block.vector
            def _(v):
                for f in self.prog["vector"]:
                    f(v)

            @block.scalar
            def _(s):
                for f in self.prog["scalar"]:
                    f(s)

            @block.sync
            def _(sy):
                for f in self.prog["sync"]:
                    f(sy)

            @block.gpsimd
            def _(g):
                for f in self.prog["gpsimd"]:
                    f(g)
        self.prog = {e: [] for e in ENGS}


def alloc_ffn_tensors(nc, es, pfx=""):
    def sb(name, shape, dtype, **kw):
        return es.enter_context(nc.sbuf_tensor(pfx + name, shape, dtype, **kw))
    t = {}
    t["x_h"] = sb("x_h", [128, NJ, D], F32)
    t["lnxT"] = sb("lnxT", [128, ND, TH], BF16, is_transpose_dest=True)
    t["hT"] = sb("hT", [128, 2, NI, TH], BF16)
    t["w13r"] = sb("w13r", [128, RING13, 2, ND, 128], BF16)
    t["w2r"] = sb("w2r", [128, NI, 512], BF16)
    t["lnbf"] = sb("lnbf", [128, 2, D], BF16)
    t["sil"] = sb("sil", [128, 2, 2, 512], F32)
    t["scr"] = sb("scr", [128, D], F32)
    t["stat"] = sb("stat", [128, 8, NJ], F32)
    t["sgate"] = sb("sgate", [128, NJ, 16], F32)
    t["cw05"] = sb("cw05", [128, NJ, NU], F32)
    t["gt"] = sb("gt", [128, 8, 16], F32)
    t["gsml"] = sb("gsml", [128, 8], F32)
    return t


def emit_ln_pass(mk, t, x_src_t, half, lnxT, x_h):
    stat, scr, lnbf = t["stat"], t["scr"], t["lnbf"]
    xr = x_src_t.rearrange("(j p) d -> p j d", p=128)

    xload_done = []
    xload_sem = []
    mk.wait_stores("sync")
    for g in range(2):
        j0 = half * NJ + g * 4
        sem = "xa" if g == 0 else "xb"
        v = mk.op("sync", sem, 16, lambda e, g=g, j0=j0: e.dma_start(
            x_h[:, g * 4:(g + 1) * 4, :], xr[:, j0:j0 + 4, :]))
        xload_done.append(v)
        xload_sem.append(sem)

    red_done = []
    for j in range(NJ):
        mk.wait("vector", xload_sem[j // 4], xload_done[j // 4])
        mk.wait("scalar", xload_sem[j // 4], xload_done[j // 4])
        mk.op("vector", "v", 1, lambda e, j=j: e.reduce_sum(
            stat[:, 0, j:j + 1], x_h[:, j, :], axis=AX.X))
        sv = mk.chain("scalar", "s", lambda e, j=j: e.activation(
            scr[:], x_h[:, j, :], AF.Square, accum_out=stat[:, 1, j:j + 1]))
        red_done.append(sv)

    mk.wait("vector", "s", red_done[-1])
    mk.chain("vector", "v", lambda e: e.tensor_scalar(
        stat[:, 2, :], stat[:, 0, :], 1.0 / D, None, ALU.mult))
    mk.chain("vector", "v", lambda e: e.tensor_scalar(
        stat[:, 3, :], stat[:, 1, :], 1.0 / D, None, ALU.mult))
    mk.chain("vector", "v", lambda e: e.tensor_tensor(
        stat[:, 4, :], stat[:, 2, :], stat[:, 2, :], ALU.mult))
    vv = mk.chain("vector", "v", lambda e: e.tensor_sub(
        stat[:, 4, :], stat[:, 3, :], stat[:, 4, :]))
    mk.wait("scalar", "v", vv)
    mk.chain("scalar", "s", lambda e: e.activation(
        stat[:, 7, :], stat[:, 4, :], AF.Ln, bias=EPS))
    sv = mk.chain("scalar", "s", lambda e: e.activation(
        stat[:, 5, :], stat[:, 7, :], AF.Exp, scale=-0.5))
    mk.wait("vector", "s", sv)
    mk.chain("vector", "v", lambda e: e.tensor_tensor(
        stat[:, 6, :], stat[:, 2, :], stat[:, 5, :], ALU.mult))
    mk.chain("vector", "v", lambda e: e.tensor_scalar(
        stat[:, 6, :], stat[:, 6, :], -1.0, None, ALU.mult))

    tr_done = []
    for j in range(NJ):
        par = j % 2
        lsem = "lta" if par == 0 else "ltb"
        if j >= 2:
            mk.wait("vector", lsem, tr_done[j - 2])
        emitfn = mk.chain if j == 0 else (lambda eng, nm, f: mk.op(eng, nm, 1, f))
        vv = emitfn("vector", "v", lambda e, j=j, par=par: e.tensor_scalar(
            lnbf[:, par, :], x_h[:, j, :], stat[:, 5, j:j + 1], stat[:, 6, j:j + 1],
            ALU.mult, ALU.add))
        mk.wait("sync", "v", vv)
        tv = mk.op("sync", lsem, 16, lambda e, j=j, par=par: e.dma_start(
            lnxT[:, :, j * 128:(j + 1) * 128], lnbf[:, par, :], transpose=True))
        tr_done.append(tv)
    return tr_done


def emit_gate(mk, t, gwt_s, eb_s, p2, lnxT_tr_done):
    lnxT, sgate, cw05, gt, gsml = t["lnxT"], t["sgate"], t["cw05"], t["gt"], t["gsml"]
    mk.wait("tensor", "lta", mk.cnt["lta"])
    mk.wait("tensor", "ltb", mk.cnt["ltb"])
    sig_done = [0, 0]
    gate_mm_done = [0, 0]
    for j in range(NJ):
        q = j % 2
        pg = p2[:, q * 512:q * 512 + 16]
        if j >= 2:
            mk.wait("tensor", "s", sig_done[q])
        else:
            mk.wait_bank_free("tensor", ("p2", q))
        for d in range(ND):
            if d < ND - 1:
                mk.raw("tensor", lambda e, d=d, j=j, pg=pg: e.matmul(
                    pg, lnxT[:, d, j * 128:(j + 1) * 128], gwt_s[:, d, :],
                    start=(d == 0), stop=False))
            else:
                gate_mm_done[q] = mk.op("tensor", "t", 1, lambda e, d=d, j=j, pg=pg: e.matmul(
                    pg, lnxT[:, d, j * 128:(j + 1) * 128], gwt_s[:, d, :],
                    start=False, stop=True))
        mk.wait("scalar", "t", gate_mm_done[q])
        sig_done[q] = mk.op("scalar", "s", 1, lambda e, j=j, pg=pg: e.activation(
            sgate[:, j, :], pg, AF.Sigmoid))

        mk.wait("vector", "s", sig_done[q])
        mk.chain("vector", "v", lambda e, j=j: e.tensor_tensor(
            gt[:, 0, :], sgate[:, j, :], eb_s[:], ALU.add))
        mk.chain("vector", "v", lambda e: e.reduce_max(
            gsml[:, 0:4], gt[:, 0, :].rearrange("p (a b) -> p a b", a=4), axis=AX.X))
        mk.chain("vector", "v", lambda e: e.reduce_max(
            gsml[:, 4:5], gsml[:, 0:4], axis=AX.X))
        mk.chain("vector", "v", lambda e: e.tensor_scalar(
            gt[:, 7, 0:4], gsml[:, 0:4], gsml[:, 4:5], None, ALU.is_ge))
        mk.chain("vector", "v", lambda e: e.scalar_tensor_tensor(
            gt[:, 7, 4:8], gt[:, 7, 0:4], NEG_BIG, gsml[:, 0:4], ALU.mult, ALU.add))
        mk.chain("vector", "v", lambda e: e.reduce_max(
            gsml[:, 4:5], gt[:, 7, 4:8], axis=AX.X))
        mk.chain("vector", "v", lambda e: e.tensor_scalar(
            gt[:, 7, 0:4], gsml[:, 0:4], gsml[:, 4:5], None, ALU.is_ge))
        mk.chain("vector", "v", lambda e: e.tensor_copy(
            gt[:, 1, :].rearrange("p (a b) -> p a b", a=4),
            gt[:, 7, 0:4].unsqueeze(2).broadcast_to((128, 4, 4))))
        mk.chain("vector", "v", lambda e: e.tensor_tensor(
            gt[:, 2, :], gt[:, 0, :], gt[:, 1, :], ALU.mult))
        mk.chain("vector", "v", lambda e: e.tensor_scalar(
            gt[:, 3, :], gt[:, 1, :], 1.0, None, ALU.subtract))
        mk.chain("vector", "v", lambda e: e.scalar_tensor_tensor(
            gt[:, 3, :], gt[:, 3, :], BIG, gt[:, 2, :], ALU.mult, ALU.add))
        mk.chain("vector", "v", lambda e: e.tensor_copy(gt[:, 4, :], gt[:, 3, :]))
        for k in range(3):
            a = 4 + (k % 2)
            b = 4 + ((k + 1) % 2)
            mk.chain("vector", "v", lambda e, a=a: e.reduce_max(
                gsml[:, 4:5], gt[:, a, :], axis=AX.X))
            mk.chain("vector", "v", lambda e, a=a: e.tensor_scalar(
                gt[:, 6, :], gt[:, a, :], gsml[:, 4:5], None, ALU.is_ge))
            mk.chain("vector", "v", lambda e, a=a, b=b: e.scalar_tensor_tensor(
                gt[:, b, :], gt[:, 6, :], NEG_BIG, gt[:, a, :], ALU.mult, ALU.add))
        mk.chain("vector", "v", lambda e: e.reduce_max(
            gsml[:, 4:5], gt[:, 5, :], axis=AX.X))
        mk.chain("vector", "v", lambda e: e.tensor_scalar(
            gt[:, 6, :], gt[:, 3, :], gsml[:, 4:5], None, ALU.is_ge))
        mk.chain("vector", "v", lambda e, j=j: e.tensor_tensor(
            gt[:, 6, :], sgate[:, j, :], gt[:, 6, :], ALU.mult))
        mk.chain("vector", "v", lambda e: e.reduce_sum(
            gsml[:, 5:6], gt[:, 6, :], axis=AX.X))
        mk.chain("vector", "v", lambda e: e.reciprocal(gsml[:, 6:7], gsml[:, 5:6]))
        mk.chain("vector", "v", lambda e: e.tensor_scalar(
            gsml[:, 6:7], gsml[:, 6:7], 0.5, None, ALU.mult))
        mk.chain("vector", "v", lambda e, j=j: e.tensor_scalar(
            cw05[:, j, 0:16], gt[:, 6, :], gsml[:, 6:7], None, ALU.mult))
    gv = mk.op("vector", "v", 1, lambda e: e.memset(cw05[:, :, 16:18], 0.5))
    mk.bank_free[("p2", 0)] = ("s", sig_done[0])
    mk.bank_free[("p2", 1)] = ("s", sig_done[1])
    return gv


def emit_ffn_units(mk, t, W13, W2, p1, p2, half, cw_ready, x_dst_t, first_half_of_program):
    x_h, lnxT, hT, w13r, w2r, sil, cw05 = (
        t["x_h"], t["lnxT"], t["hT"], t["w13r"], t["w2r"], t["sil"], t["cw05"])

    W13r = [[W13[u, m] for m in range(2)] for u in range(NU)]

    mk.call_seq += 1
    cs = mk.call_seq
    for u in range(NU):
        for i in range(NI):
            mk.w13_jobs.append((cs, u, i, W13r))
        for dq in range(4):
            for bi, (s0, bn) in enumerate(((0, 4), (4, 4), (8, 3))):
                mk.w2_jobs.append((cs, u, dq, bi, s0, bn, W2))

    def pump_w13():
        if not mk.w13_jobs:
            return
        jcs, u, i, Wr = mk.w13_jobs.popleft()
        slot = mk.w13_seq % RING13
        mk.w13_seq += 1
        mk.wait("gpsimd", "t", mk.w13_slot_free[slot])
        for m in range(2):
            v = mk.op("gpsimd", "wr%d" % slot, 16, lambda e, u=u, m=m, i=i, slot=slot, Wr=Wr: e.dma_start(
                w13r[:, slot, m], Wr[u][m][i]))
        mk.w13_loaded[(jcs, u, i)] = (slot, v)

    def pump_w2():
        if not mk.w2_jobs:
            return
        jcs, u, dq, bi, s0, bn, W2a = mk.w2_jobs.popleft()
        mk.wait("gpsimd", "t", mk.w2_slot_free[bi])
        v = mk.op("gpsimd", "wb%d" % bi, 16, lambda e, u=u, dq=dq, s0=s0, bn=bn, W2a=W2a: e.dma_start(
            w2r[:, s0:s0 + bn, :], W2a[u, dq, :, s0:s0 + bn, :]))
        mk.w2_loaded[(jcs, u, dq, bi)] = v
        mk.w2_emit_seq += 1

    while mk.w13_seq < mk.w13_seq2 + RING13 and mk.w13_jobs:
        pump_w13()
    while mk.w2_emit_seq < mk.w2_cons_seq + 3 and mk.w2_jobs:
        pump_w2()

    hT_ready = {}
    sil_free = mk.sil_free
    unit_final_stt = {}

    cw_waited = [False]

    for u in range(NU):
        ubuf = u % 2
        for i in range(NI):
            slot = mk.w13_seq2 % RING13
            mk.w13_seq2 += 1
            _wslot, _wval = mk.w13_loaded[(cs, u, i)]
            mk.wait("tensor", "wr%d" % _wslot, _wval)
            par = i % 2
            last_mm = 0
            for m in range(2):
                for t2 in range(2):
                    bank = ("p1", m * 2 + t2)
                    mk.wait_bank_free("tensor", bank)
                    for d in range(ND):
                        if d < ND - 1:
                            mk.raw("tensor", lambda e, slot=slot, m=m, t2=t2, d=d: e.matmul(
                                p1[:, (m * 2 + t2) * 512:(m * 2 + t2 + 1) * 512],
                                w13r[:, slot, m, d, :], lnxT[:, d, t2 * 512:(t2 + 1) * 512],
                                start=(d == 0), stop=False))
                        else:
                            last_mm = mk.op("tensor", "t", 1, lambda e, slot=slot, m=m, t2=t2, d=d: e.matmul(
                                p1[:, (m * 2 + t2) * 512:(m * 2 + t2 + 1) * 512],
                                w13r[:, slot, m, d, :], lnxT[:, d, t2 * 512:(t2 + 1) * 512],
                                start=False, stop=True))
                    mk.bank_ready[bank] = ("t", last_mm)
            mk.w13_slot_free[slot] = last_mm
            pump_w13()

            for t2 in range(2):
                mk.wait_bank_ready("scalar", ("p1", t2))
                mk.wait("scalar", "v", sil_free[par][t2])
                sv = mk.op("scalar", "s", 1, lambda e, par=par, t2=t2: e.activation(
                    sil[:, par, t2, :], p1[:, t2 * 512:(t2 + 1) * 512], AF.Silu))
                mk.bank_free[("p1", t2)] = ("s", sv)
                mk.wait_bank_ready("vector", ("p1", 2 + t2))
                mk.wait("vector", "s", sv)
                vv = mk.op("vector", "v", 1, lambda e, ubuf=ubuf, i=i, par=par, t2=t2: e.tensor_tensor(
                    hT[:, ubuf, i, t2 * 512:(t2 + 1) * 512], sil[:, par, t2, :],
                    p1[:, (2 + t2) * 512:(3 + t2) * 512], ALU.mult))
                mk.bank_free[("p1", 2 + t2)] = ("v", vv)
                sil_free[par][t2] = vv
            hT_ready[(u, i)] = vv

        if not cw_waited[0]:
            mk.wait("vector", "v", cw_ready)
            cw_waited[0] = True
        sub = 0
        for dq in range(4):
            for jg in range(2):
                pset, pname = (p2, "p2") if sub % 2 == 0 else (p1, "p1")
                sub += 1
                last_mm = 0
                for jj in range(4):
                    mk.wait_bank_free("tensor", (pname, jj))
                for i in range(NI):
                    bi = 0 if i < 4 else (1 if i < 8 else 2)
                    mk.wait("tensor", "wb%d" % bi, mk.w2_loaded[(cs, u, dq, bi)])
                    if dq == 0 and jg == 0:
                        mk.wait("tensor", "v", hT_ready[(u, i)])
                    boundary = (jg == 1 and i in (3, 7, 10))
                    for jj in range(4):
                        j = jg * 4 + jj
                        def _mm(e, pset=pset, ubuf=ubuf, i=i, j=j, jj=jj):
                            return e.matmul(
                                pset[:, jj * 512:(jj + 1) * 512],
                                hT[:, ubuf, i, j * 128:(j + 1) * 128], w2r[:, i, :],
                                start=(i == 0), stop=(i == NI - 1))
                        if jj == 3 and (boundary or i == NI - 1):
                            lm = mk.op("tensor", "t", 1, _mm)
                            if i == NI - 1:
                                last_mm = lm
                        else:
                            mk.raw("tensor", _mm)
                    if boundary:
                        mk.w2_slot_free[{3: 0, 7: 1, 10: 2}[i]] = lm
                        mk.w2_cons_seq += 1
                        pump_w2()
                for jj in range(4):
                    j = jg * 4 + jj
                    mk.wait("vector", "t", last_mm)
                    vv = mk.op("vector", "v", 1, lambda e, pset=pset, jj=jj, j=j, dq=dq, u=u: e.scalar_tensor_tensor(
                        x_h[:, j, dq * 512:(dq + 1) * 512], pset[:, jj * 512:(jj + 1) * 512],
                        cw05[:, j, u:u + 1], x_h[:, j, dq * 512:(dq + 1) * 512],
                        ALU.mult, ALU.add))
                    mk.bank_free[(pname, jj)] = ("v", vv)
                    if u == NU - 1 and dq == 3:
                        unit_final_stt[jg] = vv

    xdr = x_dst_t.rearrange("(j p) d -> p j d", p=128)
    store_done = []
    for jg in range(2):
        sem = "xsa" if jg == 0 else "xsb"
        mk.wait("sync", "v", unit_final_stt[jg])
        v = mk.op("sync", sem, 16, lambda e, jg=jg, half=half: e.dma_start(
            xdr[:, half * NJ + jg * 4: half * NJ + jg * 4 + 4, :],
            x_h[:, jg * 4:(jg + 1) * 4, :]))
        store_done.append(v)
    return store_done


def emit_ffn_call(mk, t, x_src_t, x_dst_t, W13, W2, gwt_s, eb_s, p1, p2):
    stores = []
    for half in range(2):
        tr_done = emit_ln_pass(mk, t, x_src_t, half, t["lnxT"], t["x_h"])
        cw_ready = emit_gate(mk, t, gwt_s, eb_s, p2, tr_done)
        st = emit_ffn_units(mk, t, W13, W2, p1, p2, half, cw_ready, x_dst_t, half == 0)
        stores += st
    return stores



NT = 16


def alloc_attn_tensors(nc, es):
    a = {}
    a["ln2T"] = es.enter_context(nc.sbuf_tensor("ln2T", [128, ND, TT], BF16, is_transpose_dest=True))
    a["wq"] = es.enter_context(nc.sbuf_tensor("wq", [128, ND, 1024], BF16))
    a["u4"] = es.enter_context(nc.sbuf_tensor("u4", [128, 6144], F32))
    a["lnbf2"] = es.enter_context(nc.sbuf_tensor("lnbf2", [128, 2, D], BF16))
    a["stat2"] = es.enter_context(nc.sbuf_tensor("stat2", [128, 8, NT], F32))
    a["scr2"] = es.enter_context(nc.sbuf_tensor("scr2", [128, 4, 1024], F32))
    a["xnb"] = es.enter_context(nc.sbuf_tensor("xnb", [128, 4, 1024], BF16))
    a["xnT"] = es.enter_context(nc.sbuf_tensor("xnT", [128, 4, 8, 128], BF16, is_transpose_dest=True))
    a["xstr"] = es.enter_context(nc.sbuf_tensor("xstr", [128, 4, 8, 128], F32))
    a["gst"] = es.enter_context(nc.sbuf_tensor("gst", [128, 8, 64], F32))
    a["rtmp"] = es.enter_context(nc.sbuf_tensor("rtmp", [128, 3, 64], F32))
    a["hhb"] = es.enter_context(nc.sbuf_tensor("hhb", [128, D], F32))
    a["gnwT"] = es.enter_context(nc.sbuf_tensor("gnwT", [128, 16], F32))
    a["gnbT"] = es.enter_context(nc.sbuf_tensor("gnbT", [128, 16], F32))
    a["cosb"] = es.enter_context(nc.sbuf_tensor("cosb", [128, NT, 64], F32))
    a["sinb"] = es.enter_context(nc.sbuf_tensor("sinb", [128, NT, 64], F32))
    return a


def emit_attn(mk, a, x_src_t, x_dst_t, WQ, SMA, p1, p2):
    ln2T, wq, lnbf2, stat2, scr2 = a["ln2T"], a["wq"], a["lnbf2"], a["stat2"], a["scr2"]
    xnb, xnT, xstr, gst, rtmp = a["xnb"], a["xnT"], a["xstr"], a["gst"], a["rtmp"]
    xtmp = a["u4"][:, 0:6144].rearrange("p (r d) -> p r d", r=3)
    Hb = a["u4"][:, 0:6144].rearrange("p (r d) -> p r d", r=6)

    sm_done = mk.op("gpsimd", "pre", 16, lambda e: e.dma_start(a["hhb"][:], SMA[:, 0:D]))
    sm_done = mk.op("gpsimd", "pre", 16, lambda e: e.dma_start(a["gnwT"][:], SMA[:, D:D + 16]))
    sm_done = mk.op("gpsimd", "pre", 16, lambda e: e.dma_start(a["gnbT"][:], SMA[:, D + 16:D + 32]))
    sm_done = mk.op("gpsimd", "pre", 16, lambda e: e.dma_start(
        a["cosb"][:], SMA[:, D + 32:D + 32 + NT * 64].rearrange("p (j c) -> p j c", c=64)))
    sm_done = mk.op("gpsimd", "pre", 16, lambda e: e.dma_start(
        a["sinb"][:], SMA[:, D + 32 + NT * 64:D + 32 + 2 * NT * 64].rearrange("p (j c) -> p j c", c=64)))

    xr = x_src_t.rearrange("(j p) d -> p j d", p=128)
    xdr = x_dst_t.rearrange("(j p) d -> p j d", p=128)

    MAGIC = 0x5F3759DF
    norm_done = [0] * NT
    redv_done = [0] * NT
    reds_done = [0] * NT
    tr_done = [0] * NT
    for j in range(NT):
        slot = j % 3
        xsem = ("xa", "xb", "xc")[slot]
        lsem = "lta" if j % 2 == 0 else "ltb"
        if j >= 3:
            mk.wait("sync", "v", norm_done[j - 3])
        ld = mk.op("sync", xsem, 16, lambda e, j=j, slot=slot: e.dma_start(
            xtmp[:, slot, :], xr[:, j, :]))
        mk.wait("vector", xsem, ld)
        mk.wait("scalar", xsem, ld)
        redv_done[j] = mk.op("vector", "v", 1, lambda e, j=j, slot=slot: e.reduce_sum(
            stat2[:, 0, j:j + 1], xtmp[:, slot, :], axis=AX.X))
        sv = mk.chain("scalar", "s", lambda e, j=j, slot=slot: e.activation(
            scr2[:, 0, :], xtmp[:, slot, 0:1024], AF.Square, accum_out=stat2[:, 2, j:j + 1]))
        sv = mk.chain("scalar", "s", lambda e, j=j, slot=slot: e.activation(
            scr2[:, 0, :], xtmp[:, slot, 1024:2048], AF.Square, accum_out=stat2[:, 3, j:j + 1]))
        reds_done[j] = sv
        if j % 2 == 0:
            continue
        jj = j - 1
        mk.wait("vector", "s", sv)
        mk.chain("vector", "v", lambda e, jj=jj: e.tensor_add(
            stat2[:, 1, jj:jj + 2], stat2[:, 2, jj:jj + 2], stat2[:, 3, jj:jj + 2]))
        mk.chain("vector", "v", lambda e, jj=jj: e.tensor_scalar(
            stat2[:, 2:4:1, jj:jj + 2] if False else stat2[:, 2, jj:jj + 2],
            stat2[:, 0, jj:jj + 2], 1.0 / D, None, ALU.mult))
        mk.chain("vector", "v", lambda e, jj=jj: e.tensor_scalar(
            stat2[:, 3, jj:jj + 2], stat2[:, 1, jj:jj + 2], 1.0 / D, None, ALU.mult))
        mk.chain("vector", "v", lambda e, jj=jj: e.tensor_tensor(
            stat2[:, 4, jj:jj + 2], stat2[:, 2, jj:jj + 2], stat2[:, 2, jj:jj + 2], ALU.mult))
        mk.chain("vector", "v", lambda e, jj=jj: e.tensor_sub(
            stat2[:, 4, jj:jj + 2], stat2[:, 3, jj:jj + 2], stat2[:, 4, jj:jj + 2]))
        mk.chain("vector", "v", lambda e, jj=jj: e.tensor_scalar(
            stat2[:, 7, jj:jj + 2], stat2[:, 4, jj:jj + 2], EPS, None, ALU.add))
        mk.chain("vector", "v", lambda e, jj=jj: e.tensor_scalar(
            stat2[:, 5, jj:jj + 2].bitcast(I32), stat2[:, 7, jj:jj + 2].bitcast(I32), 1, None,
            ALU.logical_shift_right))
        mk.chain("vector", "v", lambda e, jj=jj: e.tensor_scalar(
            stat2[:, 5, jj:jj + 2].bitcast(I32), stat2[:, 5, jj:jj + 2].bitcast(I32), -1, MAGIC,
            ALU.mult, ALU.add))
        for _ in range(3):
            mk.chain("vector", "v", lambda e, jj=jj: e.tensor_tensor(
                stat2[:, 6, jj:jj + 2], stat2[:, 5, jj:jj + 2], stat2[:, 5, jj:jj + 2], ALU.mult))
            mk.chain("vector", "v", lambda e, jj=jj: e.tensor_tensor(
                stat2[:, 6, jj:jj + 2], stat2[:, 6, jj:jj + 2], stat2[:, 7, jj:jj + 2], ALU.mult))
            mk.chain("vector", "v", lambda e, jj=jj: e.tensor_scalar(
                stat2[:, 6, jj:jj + 2], stat2[:, 6, jj:jj + 2], -0.5, 1.5, ALU.mult, ALU.add))
            mk.chain("vector", "v", lambda e, jj=jj: e.tensor_tensor(
                stat2[:, 5, jj:jj + 2], stat2[:, 5, jj:jj + 2], stat2[:, 6, jj:jj + 2], ALU.mult))
        mk.chain("vector", "v", lambda e, jj=jj: e.scalar_tensor_tensor(
            stat2[:, 6, jj:jj + 2], stat2[:, 2, jj:jj + 2], -1.0, stat2[:, 5, jj:jj + 2],
            ALU.mult, ALU.mult))
        for jn in (jj, j):
            nslot = jn % 3
            nlsem = "lta" if jn % 2 == 0 else "ltb"
            npar = jn % 2
            if jn >= 2:
                mk.wait("vector", nlsem, tr_done[jn - 2])
            norm_done[jn] = mk.chain("vector", "v", lambda e, jn=jn, nslot=nslot, npar=npar: e.tensor_scalar(
                lnbf2[:, npar, :], xtmp[:, nslot, :], stat2[:, 5, jn:jn + 1], stat2[:, 6, jn:jn + 1],
                ALU.mult, ALU.add))
            mk.wait("sync", "v", norm_done[jn])
            tr_done[jn] = mk.op("sync", nlsem, 16, lambda e, jn=jn, npar=npar: e.dma_start(
                ln2T[:, :, jn * 128:(jn + 1) * 128], lnbf2[:, npar, :], transpose=True))

    mk.wait("vector", "pre", sm_done)
    mk.wait("gpsimd", "pre", sm_done)
    stripe_ops = {}
    tanh_done = [0, 0]
    gsq_done = [0, 0, 0, 0]
    slices_last = [0, 0]
    qmm_of = {}
    relu_of = {}
    state = {"last_store": 0, "qmm": 0}

    def emitQ(ch, j):
        par = j % 2
        pset = p1 if (j % 4) < 2 else p2
        pq = pset[:, par * 1024:(par + 1) * 1024]
        mk.wait("tensor", "lta" if par == 0 else "ltb", tr_done[j])
        mk.wait_bank_free("tensor", ("pq", j % 4))
        for d in range(ND):
            for c2 in range(2):
                fn = lambda e, d=d, c2=c2, j=j, pq=pq: e.matmul(
                    pq[:, c2 * 512:(c2 + 1) * 512], ln2T[:, d, j * 128:(j + 1) * 128],
                    wq[:, d, c2 * 512:(c2 + 1) * 512],
                    start=(d == 0), stop=(d == ND - 1))
                if d == ND - 1 and c2 == 1:
                    state["qmm"] = mk.op("tensor", "t", 1, fn)
                else:
                    mk.raw("tensor", fn)
        qmm_of[(ch, j)] = state["qmm"]
        mk.bank_ready[("pq", j % 4)] = ("t", state["qmm"])

    def emitA(ch, j):
        par = j % 2
        pset = p1 if (j % 4) < 2 else p2
        pq = pset[:, par * 1024:(par + 1) * 1024]
        H = Hb[:, par, :]
        mk.wait("vector", "t", qmm_of[(ch, j)])
        if ch == 0:
            mk.chain("vector", "v", lambda e, pq=pq: e.tensor_scalar(
                rtmp[:, 0, 0:64:2], pq[:, 1:64:2], -1.0, None, ALU.mult))
            mk.chain("vector", "v", lambda e, pq=pq: e.tensor_copy(
                rtmp[:, 0, 1:64:2], pq[:, 0:64:2]))
            mk.chain("vector", "v", lambda e, j=j: e.tensor_tensor(
                rtmp[:, 1, :], rtmp[:, 0, :], a["sinb"][:, j, :], ALU.mult))
            mk.chain("vector", "v", lambda e, pq=pq, j=j: e.tensor_tensor(
                rtmp[:, 2, :], pq[:, 0:64], a["cosb"][:, j, :], ALU.mult))
            mk.chain("vector", "v", lambda e, pq=pq: e.tensor_add(
                pq[:, 0:64], rtmp[:, 1, :], rtmp[:, 2, :]))
        mk.wait("vector", "s", tanh_done[par])
        rd = mk.chain("vector", "v", lambda e, pq=pq, ch=ch, H=H: e.scalar_tensor_tensor(
            H, pq[:], 0.0, a["hhb"][:, ch * 1024:(ch + 1) * 1024], ALU.max, ALU.add))
        mk.bank_free[("pq", j % 4)] = ("v", rd)
        relu_of[(ch, j)] = rd

    def emitB1(ch, j):
        par = j % 2
        q4 = j % 4
        H = Hb[:, par, :]
        Ts = Hb[:, 2 + q4, :]
        go = (j % 4) * 16
        mk.wait("scalar", "v", relu_of[(ch, j)])
        mk.wait("scalar", "v", gsq_done[q4])
        tv = mk.chain("scalar", "s", lambda e, H=H, Ts=Ts: e.activation(Ts, H, AF.Tanh))
        tanh_done[par] = tv
        sq = mk.chain("scalar", "s", lambda e, Ts=Ts, q4=q4: e.activation(
            scr2[:, q4, :], Ts, AF.Square))
        mk.wait("vector", "s", sq)
        mk.chain("vector", "v", lambda e, Ts=Ts, go=go: e.reduce_sum(
            gst[:, 0, go:go + 16], Ts.rearrange("p (g c) -> p g c", c=64), axis=AX.X))
        gq = mk.chain("vector", "v", lambda e, q4=q4, go=go: e.reduce_sum(
            gst[:, 1, go:go + 16], scr2[:, q4, :].rearrange("p (g c) -> p g c", c=64), axis=AX.X))
        gsq_done[q4] = gq

    MAGIC = 0x5F3759DF

    def emitB2(ch, jodd):
        ps = ((jodd // 2) % 2)
        s0 = ps * 32
        sl32 = slice(s0, s0 + 32)
        mk.wait("vector", "s", slices_last[ps])
        mk.chain("vector", "v", lambda e: e.tensor_scalar(
            gst[:, 2:4, sl32], gst[:, 0:2, sl32], 1.0 / 64, None, ALU.mult))
        mk.chain("vector", "v", lambda e: e.tensor_tensor(
            gst[:, 4, sl32], gst[:, 2, sl32], gst[:, 2, sl32], ALU.mult))
        mk.chain("vector", "v", lambda e: e.tensor_sub(
            gst[:, 4, sl32], gst[:, 3, sl32], gst[:, 4, sl32]))
        mk.chain("vector", "v", lambda e: e.tensor_scalar(
            gst[:, 7, sl32], gst[:, 4, sl32], EPS, None, ALU.add))
        mk.chain("vector", "v", lambda e: e.tensor_scalar(
            gst[:, 5, sl32].bitcast(I32), gst[:, 7, sl32].bitcast(I32), 1, None,
            ALU.logical_shift_right))
        mk.chain("vector", "v", lambda e: e.tensor_scalar(
            gst[:, 5, sl32].bitcast(I32), gst[:, 5, sl32].bitcast(I32), -1, MAGIC,
            ALU.mult, ALU.add))
        for _ in range(3):
            mk.chain("vector", "v", lambda e: e.tensor_tensor(
                gst[:, 3, sl32], gst[:, 5, sl32], gst[:, 5, sl32], ALU.mult))
            mk.chain("vector", "v", lambda e: e.tensor_tensor(
                gst[:, 3, sl32], gst[:, 3, sl32], gst[:, 7, sl32], ALU.mult))
            mk.chain("vector", "v", lambda e: e.tensor_scalar(
                gst[:, 3, sl32], gst[:, 3, sl32], -0.5, 1.5, ALU.mult, ALU.add))
            mk.chain("vector", "v", lambda e: e.tensor_tensor(
                gst[:, 5, sl32], gst[:, 5, sl32], gst[:, 3, sl32], ALU.mult))
        nm = mk.chain("vector", "v", lambda e: e.scalar_tensor_tensor(
            gst[:, 6, sl32], gst[:, 2, sl32], -1.0, gst[:, 5, sl32], ALU.mult, ALU.mult))
        mk.wait("scalar", "v", nm)
        for j in (jodd - 1, jodd):
            seq = ch * NT + j
            par = j % 2
            slot = seq % 4
            tsem = ("lt4", "lt5", "lt6", "lt7")[slot]
            lsem = ("xe", "xf", "xg", "xh")[slot]
            ssem = ("xsa", "xsb", "xsc", "xsd")[slot]
            Ts = Hb[:, 2 + (j % 4), :]
            go = (j % 4) * 16
            if seq - 4 in stripe_ops:
                mk.wait("scalar", tsem, stripe_ops[seq - 4][0])
            for g in range(16):
                sl = mk.op("scalar", "s", 1, lambda e, g=g, go=go, slot=slot, Ts=Ts: e.activation(
                    xnb[:, slot, g * 64:(g + 1) * 64], Ts[:, g * 64:(g + 1) * 64], AF.Identity,
                    bias=gst[:, 6, go + g:go + g + 1], scale=gst[:, 5, go + g:go + g + 1]))
            slices_last[ps] = sl
            mk.wait("sync", "s", sl)
            if seq - 4 in stripe_ops:
                mk.wait("sync", "v", stripe_ops[seq - 4][2])
            trv = mk.op("sync", tsem, 16, lambda e, slot=slot: e.dma_start(
                xnT[:, slot], xnb[:, slot, :], transpose=True))
            if seq - 4 in stripe_ops:
                mk.wait("gpsimd", ssem, stripe_ops[seq - 4][1])
            sld = mk.op("gpsimd", lsem, 16, lambda e, ch=ch, j=j, slot=slot: e.dma_start(
                xstr[:, slot], xr[:, ch * 8:(ch + 1) * 8, j * 128:(j + 1) * 128]))
            mk.wait("vector", tsem, trv)
            mk.wait("vector", lsem, sld)
            for k in range(8):
                ct = ch * 8 + k
                mk.op("vector", "v", 1, lambda e, k=k, ct=ct, slot=slot: e.tensor_scalar(
                    xstr[:, slot, k, :], xstr[:, slot, k, :], 1.0, a["gnbT"][:, ct:ct + 1],
                    ALU.mult, ALU.add))
            mk.wait("vector", "v", mk.cnt["v"])
            for k in range(8):
                ct = ch * 8 + k
                av = mk.op("vector", "v", 1, lambda e, k=k, ct=ct, slot=slot: e.scalar_tensor_tensor(
                    xstr[:, slot, k, :], xnT[:, slot, k, :], a["gnwT"][:, ct:ct + 1],
                    xstr[:, slot, k, :], ALU.mult, ALU.add))
            mk.wait("gpsimd", "v", av)
            stv = mk.op("gpsimd", ssem, 16, lambda e, ch=ch, j=j, slot=slot: e.dma_start(
                xdr[:, ch * 8:(ch + 1) * 8, j * 128:(j + 1) * 128], xstr[:, slot]))
            stripe_ops[seq] = (trv, stv, av)
            state["last_store"] = stv

    for ch in range(2):
        mk.wait("gpsimd", "t", state["qmm"])
        wq_done = mk.op("gpsimd", "wq", 16, lambda e, ch=ch: e.dma_start(
            wq[:], WQ[:, ch * 1024:(ch + 1) * 1024].rearrange("(k p) c -> p k c", p=128)))
        mk.wait("tensor", "wq", wq_done)
        for j in range(NT):
            emitQ(ch, j)
            emitA(ch, j)
            if j % 2 == 1:
                emitB1(ch, j - 1)
                emitB1(ch, j)
                emitB2(ch, j)
    last_store = state["last_store"]
    return last_store



from concourse.bass_utils import run_bass_kernel_spmd

NCORES = 8
ROPE_DIM = 64
_CACHE = {}


def _rope_tables():
    inv_freq = (1.0 / 10000.0 ** (np.arange(0, ROPE_DIM, 2, dtype=np.float32) / ROPE_DIM)).astype(np.float32)
    freqs = np.arange(TT, dtype=np.float32)[:, None] * inv_freq[None, :]
    freqs = np.repeat(freqs, 2, axis=-1)
    return np.cos(freqs).astype(np.float32), np.sin(freqs).astype(np.float32)


def build_program():
    nc = bass.Bass("TRN2", target_bir_lowering=False)
    x0 = nc.dram_tensor("x0", [TT, D], F32, kind="ExternalInput").ap()
    W13t = nc.dram_tensor("w13", [NU, 2, NI, 128, ND, 128], F32, kind="ExternalInput").ap()
    W2t = nc.dram_tensor("w2", [NU, 4, 128, NI, 512], F32, kind="ExternalInput").ap()
    GWTt = nc.dram_tensor("gwt", [D, 16], F32, kind="ExternalInput").ap()
    EBt = nc.dram_tensor("eb", [128, 16], F32, kind="ExternalInput").ap()
    WQt = nc.dram_tensor("wqd", [D, D], F32, kind="ExternalInput").ap()
    SMAt = nc.dram_tensor("sma", [128, D + 32 + 2 * NT * 64], F32, kind="ExternalInput").ap()
    Ct = nc.dram_tensor("consts", [128, 2], F32, kind="ExternalInput").ap()
    xm1 = nc.dram_tensor("xm1", [TT, D], F32, kind="Internal").ap()
    xm2 = nc.dram_tensor("xm2", [TT, D], F32, kind="Internal").ap()
    xo = nc.dram_tensor("xo", [TT, D], F32, kind="ExternalOutput").ap()

    mk = Mk(nc)
    with ExitStack() as es:
        for s in ("xa", "xb", "xc", "xd", "xsa", "xsb", "xsc", "xsd", "lta", "ltb", "ltc", "ltd",
                  "lt4", "lt5", "lt6", "lt7", "xe", "xf", "xg", "xh", "wr0", "wr1", "wr2", "wb0", "wb1", "wb2", "pre", "wq", "g", "t", "v", "s"):
            mk.new_sem(s, es)
        gwt_s = es.enter_context(nc.sbuf_tensor("gwt_s", [128, ND, 16], BF16))
        eb_s = es.enter_context(nc.sbuf_tensor("eb_s", [128, 16], F32))
        cp = es.enter_context(nc.sbuf_tensor("cp", [128, 2], F32))
        p1 = es.enter_context(nc.psum_tensor("p1", [128, 2048], F32))
        p2 = es.enter_context(nc.psum_tensor("p2", [128, 2048], F32))
        nc.const_aps.register(F32, EPS, cp[:, 0:1])
        nc.const_aps.register(F32, -0.5, cp[:, 1:2])

        with ExitStack() as es1:
            t = alloc_ffn_tensors(nc, es1, "f1")
            mk.op("gpsimd", "pre", 16, lambda e: e.dma_start(
                gwt_s[:], GWTt.rearrange("(k p) e -> p k e", p=128)))
            mk.op("gpsimd", "pre", 16, lambda e: e.dma_start(eb_s[:], EBt))
            pre = mk.op("gpsimd", "pre", 16, lambda e: e.dma_start(cp[:], Ct))
            mk.wait("tensor", "pre", pre)
            mk.wait("vector", "pre", pre)
            mk.wait("scalar", "pre", pre)
            emit_ffn_call(mk, t, x0, xm1, W13t, W2t, gwt_s, eb_s, p1, p2)
            mk.run_block()

        with ExitStack() as es2:
            a = alloc_attn_tensors(nc, es2)
            mk.wait_stores("sync")
            mk.wait_stores("gpsimd")
            emit_attn(mk, a, xm1, xm2, WQt, SMAt, p1, p2)
            mk.run_block()

        with ExitStack() as es3:
            t2 = alloc_ffn_tensors(nc, es3, "f2")
            mk.wait_stores("sync")
            mk.wait_stores("gpsimd")
            emit_ffn_call(mk, t2, xm2, xo, W13t, W2t, gwt_s, eb_s, p1, p2)
            mk.wait_stores("sync")
            mk.run_block()
    return nc


def _prep_weights(inp):
    w1, w3, w2 = inp["L0_ffn_w1"], inp["L0_ffn_w3"], inp["L0_ffn_w2"]
    sw1, sw3, sw2 = inp["L0_ffn_sw1"], inp["L0_ffn_sw3"], inp["L0_ffn_sw2"]
    w1u = np.concatenate([w1, sw1.reshape(D, 2, TI).transpose(1, 0, 2)], 0)
    w3u = np.concatenate([w3, sw3.reshape(D, 2, TI).transpose(1, 0, 2)], 0)
    w2u = np.concatenate([w2, sw2.reshape(2, TI, D)], 0)
    W13 = np.ascontiguousarray(np.stack([
        w1u.reshape(NU, ND, 128, NI, 128).transpose(0, 3, 2, 1, 4),
        w3u.reshape(NU, ND, 128, NI, 128).transpose(0, 3, 2, 1, 4)], axis=1))
    W2 = np.ascontiguousarray(w2u.reshape(NU, NI, 128, 4, 512).transpose(0, 3, 2, 1, 4))
    GWT = np.ascontiguousarray(inp["L0_ffn_gate_w"].T)
    EB = np.ascontiguousarray(np.tile(inp["L0_ffn_expert_bias"][None, :], (128, 1)))
    WQ = np.ascontiguousarray(inp["L0_attn_W_Q"])
    cos, sin = _rope_tables()
    cosl = cos.reshape(NT, 128, 64).transpose(1, 0, 2).reshape(128, NT * 64)
    sinl = sin.reshape(NT, 128, 64).transpose(1, 0, 2).reshape(128, NT * 64)
    SMA = np.ascontiguousarray(np.concatenate([
        np.tile(inp["L0_attn_hh_b"][None, :], (128, 1)),
        inp["L0_attn_gn_weight"].reshape(16, 128).T,
        inp["L0_attn_gn_bias"].reshape(16, 128).T,
        cosl, sinl], axis=1)).astype(np.float32)
    CONSTS = np.tile(np.array([[EPS, -0.5]], np.float32), (128, 1))
    return W13, W2, GWT, EB, WQ, SMA, CONSTS


def kernel(**inp):
    _ = inp.get("L0_attn_S_n")
    X = inp["L0_attn_X"]
    W13, W2, GWT, EB, WQ, SMA, CONSTS = _prep_weights(inp)
    if "nc" not in _CACHE:
        _CACHE["nc"] = build_program()
    nc = _CACHE["nc"]
    in_maps = [dict(x0=np.ascontiguousarray(X[c]), w13=W13, w2=W2, gwt=GWT, eb=EB,
                    wqd=WQ, sma=SMA, consts=CONSTS) for c in range(NCORES)]
    res = run_bass_kernel_spmd(nc, in_maps, core_ids=list(range(NCORES)))
    return np.stack([np.asarray(res.results[c]["xo"], dtype=np.float32) for c in range(NCORES)], axis=0)


# version 07
# speedup vs baseline: 1.0472x; 1.0472x over previous
import numpy as np
from contextlib import ExitStack
import concourse.bass as bass
import concourse.mybir as mybir

F32 = mybir.dt.float32
I32 = mybir.dt.int32
BF16 = mybir.dt.bfloat16
AF = mybir.ActivationFunctionType
ALU = mybir.AluOpType
AX = mybir.AxisListType

D = 2048
TI = 1408
NU = 18
NI = 11
ND = 16
TT = 2048
TH = 1024
NJ = 8
EPS = 1e-5
BIG = 1e30
NEG_BIG = -1e30
RING13 = 3

ENGS = ("tensor", "vector", "scalar", "sync", "gpsimd")


class Mk:
    def __init__(self, nc):
        self.nc = nc
        self.prog = {e: [] for e in ENGS}
        self.cnt = {}
        self.sem = {}
        self.bank_free = {}
        self.bank_ready = {}
        self.emitted_wait = {}
        self.w13_slot_free = [0] * RING13
        self.w2_slot_free = [0] * 3
        self.sil_free = [[0, 0], [0, 0]]
        self.w13_seq = 0
        self.w13_seq2 = 0
        from collections import deque
        self.w13_jobs = deque()
        self.w2_jobs = deque()
        self.w13_loaded = {}
        self.w2_loaded = {}
        self.w2_emit_seq = 0
        self.w2_cons_seq = 0
        self.call_seq = 0

    def new_sem(self, name, es):
        self.sem[name] = es.enter_context(self.nc.semaphore(name))
        self.cnt[name] = 0

    def emit(self, eng, fn):
        self.prog[eng].append(fn)

    def wait(self, eng, name, val):
        if val <= 0:
            return
        key = (eng, name)
        if val <= self.emitted_wait.get(key, -1):
            return
        self.emitted_wait[key] = val
        s = self.sem[name]
        self.emit(eng, lambda e, s=s, v=val: e.wait_ge(s, v))

    def wait_bank_free(self, eng, bank):
        if bank in self.bank_free:
            nm, v = self.bank_free[bank]
            self.wait(eng, nm, v)

    def wait_bank_ready(self, eng, bank):
        nm, v = self.bank_ready[bank]
        self.wait(eng, nm, v)

    def op(self, eng, name, n, fn):
        self.cnt[name] += n
        v = self.cnt[name]
        s = self.sem[name]
        self.emit(eng, lambda e, fn=fn, s=s, n=n: fn(e).then_inc(s, n))
        return v

    def raw(self, eng, fn):
        self.emit(eng, fn)

    def wait_stores(self, eng):
        for nm in ("xsa", "xsb", "xsc", "xsd"):
            self.wait(eng, nm, self.cnt[nm])

    def chain(self, eng, name, fn):
        self.wait(eng, name, self.cnt[name])
        return self.op(eng, name, 1, fn)

    def run_block(self):
        nc = self.nc
        with nc.Block() as block:
            @block.tensor
            def _(t):
                for f in self.prog["tensor"]:
                    f(t)

            @block.vector
            def _(v):
                for f in self.prog["vector"]:
                    f(v)

            @block.scalar
            def _(s):
                for f in self.prog["scalar"]:
                    f(s)

            @block.sync
            def _(sy):
                for f in self.prog["sync"]:
                    f(sy)

            @block.gpsimd
            def _(g):
                for f in self.prog["gpsimd"]:
                    f(g)
        self.prog = {e: [] for e in ENGS}


def alloc_ffn_tensors(nc, es, pfx=""):
    def sb(name, shape, dtype, **kw):
        return es.enter_context(nc.sbuf_tensor(pfx + name, shape, dtype, **kw))
    t = {}
    t["x_h"] = sb("x_h", [128, NJ, D], F32)
    t["lnxT"] = sb("lnxT", [128, ND, TH], BF16, is_transpose_dest=True)
    t["hT"] = sb("hT", [128, 2, NI, TH], BF16)
    t["w13r"] = sb("w13r", [128, RING13, 2, ND, 128], BF16)
    t["w2r"] = sb("w2r", [128, NI, 512], BF16)
    t["lnbf"] = sb("lnbf", [128, 2, D], BF16)
    t["sil"] = sb("sil", [128, 2, 2, 512], F32)
    t["scr"] = sb("scr", [128, D], F32)
    t["stat"] = sb("stat", [128, 8, NJ], F32)
    t["sgate"] = sb("sgate", [128, NJ, 16], F32)
    t["cw05"] = sb("cw05", [128, NJ, NU], F32)
    t["gt"] = sb("gt", [128, 8, 16], F32)
    t["gsml"] = sb("gsml", [128, 8], F32)
    return t


def emit_ln_pass(mk, t, x_src_t, half, lnxT, x_h):
    stat, scr, lnbf = t["stat"], t["scr"], t["lnbf"]
    xr = x_src_t.rearrange("(j p) d -> p j d", p=128)

    xload_done = []
    xload_sem = []
    mk.wait_stores("sync")
    for g in range(2):
        j0 = half * NJ + g * 4
        sem = "xa" if g == 0 else "xb"
        v = mk.op("sync", sem, 16, lambda e, g=g, j0=j0: e.dma_start(
            x_h[:, g * 4:(g + 1) * 4, :], xr[:, j0:j0 + 4, :]))
        xload_done.append(v)
        xload_sem.append(sem)

    red_done = []
    for j in range(NJ):
        mk.wait("vector", xload_sem[j // 4], xload_done[j // 4])
        mk.wait("scalar", xload_sem[j // 4], xload_done[j // 4])
        mk.op("vector", "v", 1, lambda e, j=j: e.reduce_sum(
            stat[:, 0, j:j + 1], x_h[:, j, :], axis=AX.X))
        sv = mk.chain("scalar", "s", lambda e, j=j: e.activation(
            scr[:], x_h[:, j, :], AF.Square, accum_out=stat[:, 1, j:j + 1]))
        red_done.append(sv)

    mk.wait("vector", "s", red_done[-1])
    mk.chain("vector", "v", lambda e: e.tensor_scalar(
        stat[:, 2, :], stat[:, 0, :], 1.0 / D, None, ALU.mult))
    mk.chain("vector", "v", lambda e: e.tensor_scalar(
        stat[:, 3, :], stat[:, 1, :], 1.0 / D, None, ALU.mult))
    mk.chain("vector", "v", lambda e: e.tensor_tensor(
        stat[:, 4, :], stat[:, 2, :], stat[:, 2, :], ALU.mult))
    vv = mk.chain("vector", "v", lambda e: e.tensor_sub(
        stat[:, 4, :], stat[:, 3, :], stat[:, 4, :]))
    mk.wait("scalar", "v", vv)
    mk.chain("scalar", "s", lambda e: e.activation(
        stat[:, 7, :], stat[:, 4, :], AF.Ln, bias=EPS))
    sv = mk.chain("scalar", "s", lambda e: e.activation(
        stat[:, 5, :], stat[:, 7, :], AF.Exp, scale=-0.5))
    mk.wait("vector", "s", sv)
    mk.chain("vector", "v", lambda e: e.tensor_tensor(
        stat[:, 6, :], stat[:, 2, :], stat[:, 5, :], ALU.mult))
    mk.chain("vector", "v", lambda e: e.tensor_scalar(
        stat[:, 6, :], stat[:, 6, :], -1.0, None, ALU.mult))

    tr_done = []
    for j in range(NJ):
        par = j % 2
        lsem = "lta" if par == 0 else "ltb"
        if j >= 2:
            mk.wait("vector", lsem, tr_done[j - 2])
        emitfn = mk.chain if j == 0 else (lambda eng, nm, f: mk.op(eng, nm, 1, f))
        vv = emitfn("vector", "v", lambda e, j=j, par=par: e.tensor_scalar(
            lnbf[:, par, :], x_h[:, j, :], stat[:, 5, j:j + 1], stat[:, 6, j:j + 1],
            ALU.mult, ALU.add))
        mk.wait("sync", "v", vv)
        tv = mk.op("sync", lsem, 16, lambda e, j=j, par=par: e.dma_start(
            lnxT[:, :, j * 128:(j + 1) * 128], lnbf[:, par, :], transpose=True))
        tr_done.append(tv)
    return tr_done


def emit_gate(mk, t, gwt_s, eb_s, p2, lnxT_tr_done):
    lnxT, sgate, cw05, gt, gsml = t["lnxT"], t["sgate"], t["cw05"], t["gt"], t["gsml"]
    mk.wait("tensor", "lta", mk.cnt["lta"])
    mk.wait("tensor", "ltb", mk.cnt["ltb"])
    sig_done = [0, 0]
    gate_mm_done = [0, 0]
    for j in range(NJ):
        q = j % 2
        pg = p2[:, q * 512:q * 512 + 16]
        if j >= 2:
            mk.wait("tensor", "s", sig_done[q])
        else:
            mk.wait_bank_free("tensor", ("p2", q))
        for d in range(ND):
            if d < ND - 1:
                mk.raw("tensor", lambda e, d=d, j=j, pg=pg: e.matmul(
                    pg, lnxT[:, d, j * 128:(j + 1) * 128], gwt_s[:, d, :],
                    start=(d == 0), stop=False))
            else:
                gate_mm_done[q] = mk.op("tensor", "t", 1, lambda e, d=d, j=j, pg=pg: e.matmul(
                    pg, lnxT[:, d, j * 128:(j + 1) * 128], gwt_s[:, d, :],
                    start=False, stop=True))
        mk.wait("scalar", "t", gate_mm_done[q])
        sig_done[q] = mk.op("scalar", "s", 1, lambda e, j=j, pg=pg: e.activation(
            sgate[:, j, :], pg, AF.Sigmoid))

        mk.wait("vector", "s", sig_done[q])
        mk.chain("vector", "v", lambda e, j=j: e.tensor_tensor(
            gt[:, 0, :], sgate[:, j, :], eb_s[:], ALU.add))
        mk.chain("vector", "v", lambda e: e.reduce_max(
            gsml[:, 0:4], gt[:, 0, :].rearrange("p (a b) -> p a b", a=4), axis=AX.X))
        mk.chain("vector", "v", lambda e: e.reduce_max(
            gsml[:, 4:5], gsml[:, 0:4], axis=AX.X))
        mk.chain("vector", "v", lambda e: e.tensor_scalar(
            gt[:, 7, 0:4], gsml[:, 0:4], gsml[:, 4:5], None, ALU.is_ge))
        mk.chain("vector", "v", lambda e: e.scalar_tensor_tensor(
            gt[:, 7, 4:8], gt[:, 7, 0:4], NEG_BIG, gsml[:, 0:4], ALU.mult, ALU.add))
        mk.chain("vector", "v", lambda e: e.reduce_max(
            gsml[:, 4:5], gt[:, 7, 4:8], axis=AX.X))
        mk.chain("vector", "v", lambda e: e.tensor_scalar(
            gt[:, 7, 0:4], gsml[:, 0:4], gsml[:, 4:5], None, ALU.is_ge))
        mk.chain("vector", "v", lambda e: e.tensor_copy(
            gt[:, 1, :].rearrange("p (a b) -> p a b", a=4),
            gt[:, 7, 0:4].unsqueeze(2).broadcast_to((128, 4, 4))))
        mk.chain("vector", "v", lambda e: e.tensor_tensor(
            gt[:, 2, :], gt[:, 0, :], gt[:, 1, :], ALU.mult))
        mk.chain("vector", "v", lambda e: e.tensor_scalar(
            gt[:, 3, :], gt[:, 1, :], 1.0, None, ALU.subtract))
        mk.chain("vector", "v", lambda e: e.scalar_tensor_tensor(
            gt[:, 3, :], gt[:, 3, :], BIG, gt[:, 2, :], ALU.mult, ALU.add))
        mk.chain("vector", "v", lambda e: e.tensor_copy(gt[:, 4, :], gt[:, 3, :]))
        for k in range(3):
            a = 4 + (k % 2)
            b = 4 + ((k + 1) % 2)
            mk.chain("vector", "v", lambda e, a=a: e.reduce_max(
                gsml[:, 4:5], gt[:, a, :], axis=AX.X))
            mk.chain("vector", "v", lambda e, a=a: e.tensor_scalar(
                gt[:, 6, :], gt[:, a, :], gsml[:, 4:5], None, ALU.is_ge))
            mk.chain("vector", "v", lambda e, a=a, b=b: e.scalar_tensor_tensor(
                gt[:, b, :], gt[:, 6, :], NEG_BIG, gt[:, a, :], ALU.mult, ALU.add))
        mk.chain("vector", "v", lambda e: e.reduce_max(
            gsml[:, 4:5], gt[:, 5, :], axis=AX.X))
        mk.chain("vector", "v", lambda e: e.tensor_scalar(
            gt[:, 6, :], gt[:, 3, :], gsml[:, 4:5], None, ALU.is_ge))
        mk.chain("vector", "v", lambda e, j=j: e.tensor_tensor(
            gt[:, 6, :], sgate[:, j, :], gt[:, 6, :], ALU.mult))
        mk.chain("vector", "v", lambda e: e.reduce_sum(
            gsml[:, 5:6], gt[:, 6, :], axis=AX.X))
        mk.chain("vector", "v", lambda e: e.reciprocal(gsml[:, 6:7], gsml[:, 5:6]))
        mk.chain("vector", "v", lambda e: e.tensor_scalar(
            gsml[:, 6:7], gsml[:, 6:7], 0.5, None, ALU.mult))
        mk.chain("vector", "v", lambda e, j=j: e.tensor_scalar(
            cw05[:, j, 0:16], gt[:, 6, :], gsml[:, 6:7], None, ALU.mult))
    gv = mk.op("vector", "v", 1, lambda e: e.memset(cw05[:, :, 16:18], 0.5))
    mk.bank_free[("p2", 0)] = ("s", sig_done[0])
    mk.bank_free[("p2", 1)] = ("s", sig_done[1])
    return gv


def emit_ffn_units(mk, t, W13, W2, p1, p2, half, cw_ready, x_dst_t, first_half_of_program):
    x_h, lnxT, hT, w13r, w2r, sil, cw05 = (
        t["x_h"], t["lnxT"], t["hT"], t["w13r"], t["w2r"], t["sil"], t["cw05"])

    W13r = [[W13[u, m] for m in range(2)] for u in range(NU)]

    mk.call_seq += 1
    cs = mk.call_seq
    for u in range(NU):
        for i in range(NI):
            mk.w13_jobs.append((cs, u, i, W13r))
        for dq in range(4):
            for bi, (s0, bn) in enumerate(((0, 4), (4, 4), (8, 3))):
                mk.w2_jobs.append((cs, u, dq, bi, s0, bn, W2))

    def pump_w13():
        if not mk.w13_jobs:
            return
        jcs, u, i, Wr = mk.w13_jobs.popleft()
        slot = mk.w13_seq % RING13
        mk.w13_seq += 1
        mk.wait("gpsimd", "t", mk.w13_slot_free[slot])
        for m in range(2):
            v = mk.op("gpsimd", "wr%d" % slot, 16, lambda e, u=u, m=m, i=i, slot=slot, Wr=Wr: e.dma_start(
                w13r[:, slot, m], Wr[u][m][i]))
        mk.w13_loaded[(jcs, u, i)] = (slot, v)

    def pump_w2():
        if not mk.w2_jobs:
            return
        jcs, u, dq, bi, s0, bn, W2a = mk.w2_jobs.popleft()
        mk.wait("gpsimd", "t", mk.w2_slot_free[bi])
        v = mk.op("gpsimd", "wb%d" % bi, 16, lambda e, u=u, dq=dq, s0=s0, bn=bn, W2a=W2a: e.dma_start(
            w2r[:, s0:s0 + bn, :], W2a[u, dq, :, s0:s0 + bn, :]))
        mk.w2_loaded[(jcs, u, dq, bi)] = v
        mk.w2_emit_seq += 1

    while mk.w13_seq < mk.w13_seq2 + RING13 and mk.w13_jobs:
        pump_w13()
    while mk.w2_emit_seq < mk.w2_cons_seq + 3 and mk.w2_jobs:
        pump_w2()

    hT_ready = {}
    sil_free = mk.sil_free
    unit_final_stt = {}
    cw_waited = [False]

    def emit_p1_block(u, i):
        ubuf = u % 2
        slot = mk.w13_seq2 % RING13
        mk.w13_seq2 += 1
        _wslot, _wval = mk.w13_loaded[(cs, u, i)]
        mk.wait("tensor", "wr%d" % _wslot, _wval)
        par = i % 2
        last_mm = 0
        for m in range(2):
            for t2 in range(2):
                bank = ("p1", m * 2 + t2)
                mk.wait_bank_free("tensor", bank)
                for d in range(ND):
                    if d < ND - 1:
                        mk.raw("tensor", lambda e, slot=slot, m=m, t2=t2, d=d: e.matmul(
                            p1[:, (m * 2 + t2) * 512:(m * 2 + t2 + 1) * 512],
                            w13r[:, slot, m, d, :], lnxT[:, d, t2 * 512:(t2 + 1) * 512],
                            start=(d == 0), stop=False))
                    else:
                        last_mm = mk.op("tensor", "t", 1, lambda e, slot=slot, m=m, t2=t2, d=d: e.matmul(
                            p1[:, (m * 2 + t2) * 512:(m * 2 + t2 + 1) * 512],
                            w13r[:, slot, m, d, :], lnxT[:, d, t2 * 512:(t2 + 1) * 512],
                            start=False, stop=True))
                mk.bank_ready[bank] = ("t", last_mm)
        mk.w13_slot_free[slot] = last_mm
        pump_w13()
        for t2 in range(2):
            mk.wait_bank_ready("scalar", ("p1", t2))
            mk.wait("scalar", "v", sil_free[par][t2])
            sv = mk.op("scalar", "s", 1, lambda e, par=par, t2=t2: e.activation(
                sil[:, par, t2, :], p1[:, t2 * 512:(t2 + 1) * 512], AF.Silu))
            mk.bank_free[("p1", t2)] = ("s", sv)
            mk.wait_bank_ready("vector", ("p1", 2 + t2))
            mk.wait("vector", "s", sv)
            vv = mk.op("vector", "v", 1, lambda e, ubuf=ubuf, i=i, par=par, t2=t2: e.tensor_tensor(
                hT[:, ubuf, i, t2 * 512:(t2 + 1) * 512], sil[:, par, t2, :],
                p1[:, (2 + t2) * 512:(3 + t2) * 512], ALU.mult))
            mk.bank_free[("p1", 2 + t2)] = ("v", vv)
            sil_free[par][t2] = vv
        hT_ready[(u, i)] = vv

    def emit_p2_block(u, dq, jg):
        ubuf = u % 2
        if not cw_waited[0]:
            mk.wait("vector", "v", cw_ready)
            cw_waited[0] = True
        last_mm = 0
        for jj in range(4):
            mk.wait_bank_free("tensor", ("p2", jj))
        for i in range(NI):
            bi = 0 if i < 4 else (1 if i < 8 else 2)
            mk.wait("tensor", "wb%d" % bi, mk.w2_loaded[(cs, u, dq, bi)])
            if dq == 0 and jg == 0:
                mk.wait("tensor", "v", hT_ready[(u, i)])
            boundary = (jg == 1 and i in (3, 7, 10))
            for jj in range(4):
                j = jg * 4 + jj
                def _mm(e, ubuf=ubuf, i=i, j=j, jj=jj):
                    return e.matmul(
                        p2[:, jj * 512:(jj + 1) * 512],
                        hT[:, ubuf, i, j * 128:(j + 1) * 128], w2r[:, i, :],
                        start=(i == 0), stop=(i == NI - 1))
                if jj == 3 and (boundary or i == NI - 1):
                    lm = mk.op("tensor", "t", 1, _mm)
                    if i == NI - 1:
                        last_mm = lm
                else:
                    mk.raw("tensor", _mm)
            if boundary:
                mk.w2_slot_free[{3: 0, 7: 1, 10: 2}[i]] = lm
                mk.w2_cons_seq += 1
                pump_w2()
        for jj in range(4):
            j = jg * 4 + jj
            mk.wait("vector", "t", last_mm)
            vv = mk.op("vector", "v", 1, lambda e, jj=jj, j=j, dq=dq, u=u: e.scalar_tensor_tensor(
                x_h[:, j, dq * 512:(dq + 1) * 512], p2[:, jj * 512:(jj + 1) * 512],
                cw05[:, j, u:u + 1], x_h[:, j, dq * 512:(dq + 1) * 512],
                ALU.mult, ALU.add))
            mk.bank_free[("p2", jj)] = ("v", vv)
            if u == NU - 1 and dq == 3:
                unit_final_stt[jg] = vv

    for i in range(NI):
        emit_p1_block(0, i)
    for u in range(NU):
        p2_blocks = [(dq, jg) for dq in range(4) for jg in range(2)]
        p1_blocks = list(range(NI)) if u + 1 < NU else []
        n2 = 0
        for k, i in enumerate(p1_blocks):
            while n2 < len(p2_blocks) and n2 < (k * 8) // 10 + 1:
                emit_p2_block(u, *p2_blocks[n2])
                n2 += 1
            emit_p1_block(u + 1, i)
        while n2 < len(p2_blocks):
            emit_p2_block(u, *p2_blocks[n2])
            n2 += 1

    xdr = x_dst_t.rearrange("(j p) d -> p j d", p=128)
    store_done = []
    for jg in range(2):
        sem = "xsa" if jg == 0 else "xsb"
        mk.wait("sync", "v", unit_final_stt[jg])
        v = mk.op("sync", sem, 16, lambda e, jg=jg, half=half: e.dma_start(
            xdr[:, half * NJ + jg * 4: half * NJ + jg * 4 + 4, :],
            x_h[:, jg * 4:(jg + 1) * 4, :]))
        store_done.append(v)
    return store_done


def emit_ffn_call(mk, t, x_src_t, x_dst_t, W13, W2, gwt_s, eb_s, p1, p2):
    stores = []
    for half in range(2):
        tr_done = emit_ln_pass(mk, t, x_src_t, half, t["lnxT"], t["x_h"])
        cw_ready = emit_gate(mk, t, gwt_s, eb_s, p2, tr_done)
        st = emit_ffn_units(mk, t, W13, W2, p1, p2, half, cw_ready, x_dst_t, half == 0)
        stores += st
    return stores



NT = 16


def alloc_attn_tensors(nc, es):
    a = {}
    a["ln2T"] = es.enter_context(nc.sbuf_tensor("ln2T", [128, ND, TT], BF16, is_transpose_dest=True))
    a["wq"] = es.enter_context(nc.sbuf_tensor("wq", [128, ND, 1024], BF16))
    a["u4"] = es.enter_context(nc.sbuf_tensor("u4", [128, 6144], F32))
    a["lnbf2"] = es.enter_context(nc.sbuf_tensor("lnbf2", [128, 2, D], BF16))
    a["stat2"] = es.enter_context(nc.sbuf_tensor("stat2", [128, 8, NT], F32))
    a["scr2"] = es.enter_context(nc.sbuf_tensor("scr2", [128, 4, 1024], F32))
    a["xnb"] = es.enter_context(nc.sbuf_tensor("xnb", [128, 4, 1024], BF16))
    a["xnT"] = es.enter_context(nc.sbuf_tensor("xnT", [128, 4, 8, 128], BF16, is_transpose_dest=True))
    a["xstr"] = es.enter_context(nc.sbuf_tensor("xstr", [128, 4, 8, 128], F32))
    a["gst"] = es.enter_context(nc.sbuf_tensor("gst", [128, 8, 64], F32))
    a["rtmp"] = es.enter_context(nc.sbuf_tensor("rtmp", [128, 3, 64], F32))
    a["hhb"] = es.enter_context(nc.sbuf_tensor("hhb", [128, D], F32))
    a["gnwT"] = es.enter_context(nc.sbuf_tensor("gnwT", [128, 16], F32))
    a["gnbT"] = es.enter_context(nc.sbuf_tensor("gnbT", [128, 16], F32))
    a["cosb"] = es.enter_context(nc.sbuf_tensor("cosb", [128, NT, 64], F32))
    a["sinb"] = es.enter_context(nc.sbuf_tensor("sinb", [128, NT, 64], F32))
    return a


def emit_attn(mk, a, x_src_t, x_dst_t, WQ, SMA, p1, p2):
    ln2T, wq, lnbf2, stat2, scr2 = a["ln2T"], a["wq"], a["lnbf2"], a["stat2"], a["scr2"]
    xnb, xnT, xstr, gst, rtmp = a["xnb"], a["xnT"], a["xstr"], a["gst"], a["rtmp"]
    xtmp = a["u4"][:, 0:6144].rearrange("p (r d) -> p r d", r=3)
    Hb = a["u4"][:, 0:6144].rearrange("p (r d) -> p r d", r=6)

    sm_done = mk.op("gpsimd", "pre", 16, lambda e: e.dma_start(a["hhb"][:], SMA[:, 0:D]))
    sm_done = mk.op("gpsimd", "pre", 16, lambda e: e.dma_start(a["gnwT"][:], SMA[:, D:D + 16]))
    sm_done = mk.op("gpsimd", "pre", 16, lambda e: e.dma_start(a["gnbT"][:], SMA[:, D + 16:D + 32]))
    sm_done = mk.op("gpsimd", "pre", 16, lambda e: e.dma_start(
        a["cosb"][:], SMA[:, D + 32:D + 32 + NT * 64].rearrange("p (j c) -> p j c", c=64)))
    sm_done = mk.op("gpsimd", "pre", 16, lambda e: e.dma_start(
        a["sinb"][:], SMA[:, D + 32 + NT * 64:D + 32 + 2 * NT * 64].rearrange("p (j c) -> p j c", c=64)))

    xr = x_src_t.rearrange("(j p) d -> p j d", p=128)
    xdr = x_dst_t.rearrange("(j p) d -> p j d", p=128)

    MAGIC = 0x5F3759DF
    norm_done = [0] * NT
    redv_done = [0] * NT
    reds_done = [0] * NT
    tr_done = [0] * NT
    for j in range(NT):
        slot = j % 3
        xsem = ("xa", "xb", "xc")[slot]
        lsem = "lta" if j % 2 == 0 else "ltb"
        if j >= 3:
            mk.wait("sync", "v", norm_done[j - 3])
        ld = mk.op("sync", xsem, 16, lambda e, j=j, slot=slot: e.dma_start(
            xtmp[:, slot, :], xr[:, j, :]))
        mk.wait("vector", xsem, ld)
        mk.wait("scalar", xsem, ld)
        redv_done[j] = mk.op("vector", "v", 1, lambda e, j=j, slot=slot: e.reduce_sum(
            stat2[:, 0, j:j + 1], xtmp[:, slot, :], axis=AX.X))
        sv = mk.chain("scalar", "s", lambda e, j=j, slot=slot: e.activation(
            scr2[:, 0, :], xtmp[:, slot, 0:1024], AF.Square, accum_out=stat2[:, 2, j:j + 1]))
        sv = mk.chain("scalar", "s", lambda e, j=j, slot=slot: e.activation(
            scr2[:, 0, :], xtmp[:, slot, 1024:2048], AF.Square, accum_out=stat2[:, 3, j:j + 1]))
        reds_done[j] = sv
        if j % 2 == 0:
            continue
        jj = j - 1
        mk.wait("vector", "s", sv)
        mk.chain("vector", "v", lambda e, jj=jj: e.tensor_add(
            stat2[:, 1, jj:jj + 2], stat2[:, 2, jj:jj + 2], stat2[:, 3, jj:jj + 2]))
        mk.chain("vector", "v", lambda e, jj=jj: e.tensor_scalar(
            stat2[:, 2:4:1, jj:jj + 2] if False else stat2[:, 2, jj:jj + 2],
            stat2[:, 0, jj:jj + 2], 1.0 / D, None, ALU.mult))
        mk.chain("vector", "v", lambda e, jj=jj: e.tensor_scalar(
            stat2[:, 3, jj:jj + 2], stat2[:, 1, jj:jj + 2], 1.0 / D, None, ALU.mult))
        mk.chain("vector", "v", lambda e, jj=jj: e.tensor_tensor(
            stat2[:, 4, jj:jj + 2], stat2[:, 2, jj:jj + 2], stat2[:, 2, jj:jj + 2], ALU.mult))
        mk.chain("vector", "v", lambda e, jj=jj: e.tensor_sub(
            stat2[:, 4, jj:jj + 2], stat2[:, 3, jj:jj + 2], stat2[:, 4, jj:jj + 2]))
        mk.chain("vector", "v", lambda e, jj=jj: e.tensor_scalar(
            stat2[:, 7, jj:jj + 2], stat2[:, 4, jj:jj + 2], EPS, None, ALU.add))
        mk.chain("vector", "v", lambda e, jj=jj: e.tensor_scalar(
            stat2[:, 5, jj:jj + 2].bitcast(I32), stat2[:, 7, jj:jj + 2].bitcast(I32), 1, None,
            ALU.logical_shift_right))
        mk.chain("vector", "v", lambda e, jj=jj: e.tensor_scalar(
            stat2[:, 5, jj:jj + 2].bitcast(I32), stat2[:, 5, jj:jj + 2].bitcast(I32), -1, MAGIC,
            ALU.mult, ALU.add))
        for _ in range(3):
            mk.chain("vector", "v", lambda e, jj=jj: e.tensor_tensor(
                stat2[:, 6, jj:jj + 2], stat2[:, 5, jj:jj + 2], stat2[:, 5, jj:jj + 2], ALU.mult))
            mk.chain("vector", "v", lambda e, jj=jj: e.tensor_tensor(
                stat2[:, 6, jj:jj + 2], stat2[:, 6, jj:jj + 2], stat2[:, 7, jj:jj + 2], ALU.mult))
            mk.chain("vector", "v", lambda e, jj=jj: e.tensor_scalar(
                stat2[:, 6, jj:jj + 2], stat2[:, 6, jj:jj + 2], -0.5, 1.5, ALU.mult, ALU.add))
            mk.chain("vector", "v", lambda e, jj=jj: e.tensor_tensor(
                stat2[:, 5, jj:jj + 2], stat2[:, 5, jj:jj + 2], stat2[:, 6, jj:jj + 2], ALU.mult))
        mk.chain("vector", "v", lambda e, jj=jj: e.scalar_tensor_tensor(
            stat2[:, 6, jj:jj + 2], stat2[:, 2, jj:jj + 2], -1.0, stat2[:, 5, jj:jj + 2],
            ALU.mult, ALU.mult))
        for jn in (jj, j):
            nslot = jn % 3
            nlsem = "lta" if jn % 2 == 0 else "ltb"
            npar = jn % 2
            if jn >= 2:
                mk.wait("vector", nlsem, tr_done[jn - 2])
            norm_done[jn] = mk.chain("vector", "v", lambda e, jn=jn, nslot=nslot, npar=npar: e.tensor_scalar(
                lnbf2[:, npar, :], xtmp[:, nslot, :], stat2[:, 5, jn:jn + 1], stat2[:, 6, jn:jn + 1],
                ALU.mult, ALU.add))
            mk.wait("sync", "v", norm_done[jn])
            tr_done[jn] = mk.op("sync", nlsem, 16, lambda e, jn=jn, npar=npar: e.dma_start(
                ln2T[:, :, jn * 128:(jn + 1) * 128], lnbf2[:, npar, :], transpose=True))

    mk.wait("vector", "pre", sm_done)
    mk.wait("gpsimd", "pre", sm_done)
    stripe_ops = {}
    tanh_done = [0, 0]
    gsq_done = [0, 0, 0, 0]
    slices_last = [0, 0]
    qmm_of = {}
    relu_of = {}
    state = {"last_store": 0, "qmm": 0}

    def emitQ(ch, j):
        par = j % 2
        pset = p1 if (j % 4) < 2 else p2
        pq = pset[:, par * 1024:(par + 1) * 1024]
        mk.wait("tensor", "lta" if par == 0 else "ltb", tr_done[j])
        mk.wait_bank_free("tensor", ("pq", j % 4))
        for d in range(ND):
            for c2 in range(2):
                fn = lambda e, d=d, c2=c2, j=j, pq=pq: e.matmul(
                    pq[:, c2 * 512:(c2 + 1) * 512], ln2T[:, d, j * 128:(j + 1) * 128],
                    wq[:, d, c2 * 512:(c2 + 1) * 512],
                    start=(d == 0), stop=(d == ND - 1))
                if d == ND - 1 and c2 == 1:
                    state["qmm"] = mk.op("tensor", "t", 1, fn)
                else:
                    mk.raw("tensor", fn)
        qmm_of[(ch, j)] = state["qmm"]
        mk.bank_ready[("pq", j % 4)] = ("t", state["qmm"])

    def emitA(ch, j):
        par = j % 2
        pset = p1 if (j % 4) < 2 else p2
        pq = pset[:, par * 1024:(par + 1) * 1024]
        H = Hb[:, par, :]
        mk.wait("vector", "t", qmm_of[(ch, j)])
        if ch == 0:
            mk.chain("vector", "v", lambda e, pq=pq: e.tensor_scalar(
                rtmp[:, 0, 0:64:2], pq[:, 1:64:2], -1.0, None, ALU.mult))
            mk.chain("vector", "v", lambda e, pq=pq: e.tensor_copy(
                rtmp[:, 0, 1:64:2], pq[:, 0:64:2]))
            mk.chain("vector", "v", lambda e, j=j: e.tensor_tensor(
                rtmp[:, 1, :], rtmp[:, 0, :], a["sinb"][:, j, :], ALU.mult))
            mk.chain("vector", "v", lambda e, pq=pq, j=j: e.tensor_tensor(
                rtmp[:, 2, :], pq[:, 0:64], a["cosb"][:, j, :], ALU.mult))
            mk.chain("vector", "v", lambda e, pq=pq: e.tensor_add(
                pq[:, 0:64], rtmp[:, 1, :], rtmp[:, 2, :]))
        mk.wait("vector", "s", tanh_done[par])
        rd = mk.chain("vector", "v", lambda e, pq=pq, ch=ch, H=H: e.scalar_tensor_tensor(
            H, pq[:], 0.0, a["hhb"][:, ch * 1024:(ch + 1) * 1024], ALU.max, ALU.add))
        mk.bank_free[("pq", j % 4)] = ("v", rd)
        relu_of[(ch, j)] = rd

    def emitB1(ch, j):
        par = j % 2
        q4 = j % 4
        H = Hb[:, par, :]
        Ts = Hb[:, 2 + q4, :]
        go = (j % 4) * 16
        mk.wait("scalar", "v", relu_of[(ch, j)])
        mk.wait("scalar", "v", gsq_done[q4])
        tv = mk.chain("scalar", "s", lambda e, H=H, Ts=Ts: e.activation(Ts, H, AF.Tanh))
        tanh_done[par] = tv
        sq = mk.chain("scalar", "s", lambda e, Ts=Ts, q4=q4: e.activation(
            scr2[:, q4, :], Ts, AF.Square))
        mk.wait("vector", "s", sq)
        mk.chain("vector", "v", lambda e, Ts=Ts, go=go: e.reduce_sum(
            gst[:, 0, go:go + 16], Ts.rearrange("p (g c) -> p g c", c=64), axis=AX.X))
        gq = mk.chain("vector", "v", lambda e, q4=q4, go=go: e.reduce_sum(
            gst[:, 1, go:go + 16], scr2[:, q4, :].rearrange("p (g c) -> p g c", c=64), axis=AX.X))
        gsq_done[q4] = gq

    MAGIC = 0x5F3759DF

    def emitB2(ch, jodd):
        ps = ((jodd // 2) % 2)
        s0 = ps * 32
        sl32 = slice(s0, s0 + 32)
        mk.wait("vector", "s", slices_last[ps])
        mk.chain("vector", "v", lambda e: e.tensor_scalar(
            gst[:, 2:4, sl32], gst[:, 0:2, sl32], 1.0 / 64, None, ALU.mult))
        mk.chain("vector", "v", lambda e: e.tensor_tensor(
            gst[:, 4, sl32], gst[:, 2, sl32], gst[:, 2, sl32], ALU.mult))
        mk.chain("vector", "v", lambda e: e.tensor_sub(
            gst[:, 4, sl32], gst[:, 3, sl32], gst[:, 4, sl32]))
        mk.chain("vector", "v", lambda e: e.tensor_scalar(
            gst[:, 7, sl32], gst[:, 4, sl32], EPS, None, ALU.add))
        mk.chain("vector", "v", lambda e: e.tensor_scalar(
            gst[:, 5, sl32].bitcast(I32), gst[:, 7, sl32].bitcast(I32), 1, None,
            ALU.logical_shift_right))
        mk.chain("vector", "v", lambda e: e.tensor_scalar(
            gst[:, 5, sl32].bitcast(I32), gst[:, 5, sl32].bitcast(I32), -1, MAGIC,
            ALU.mult, ALU.add))
        for _ in range(3):
            mk.chain("vector", "v", lambda e: e.tensor_tensor(
                gst[:, 3, sl32], gst[:, 5, sl32], gst[:, 5, sl32], ALU.mult))
            mk.chain("vector", "v", lambda e: e.tensor_tensor(
                gst[:, 3, sl32], gst[:, 3, sl32], gst[:, 7, sl32], ALU.mult))
            mk.chain("vector", "v", lambda e: e.tensor_scalar(
                gst[:, 3, sl32], gst[:, 3, sl32], -0.5, 1.5, ALU.mult, ALU.add))
            mk.chain("vector", "v", lambda e: e.tensor_tensor(
                gst[:, 5, sl32], gst[:, 5, sl32], gst[:, 3, sl32], ALU.mult))
        nm = mk.chain("vector", "v", lambda e: e.scalar_tensor_tensor(
            gst[:, 6, sl32], gst[:, 2, sl32], -1.0, gst[:, 5, sl32], ALU.mult, ALU.mult))
        mk.wait("scalar", "v", nm)
        for j in (jodd - 1, jodd):
            seq = ch * NT + j
            par = j % 2
            slot = seq % 4
            tsem = ("lt4", "lt5", "lt6", "lt7")[slot]
            lsem = ("xe", "xf", "xg", "xh")[slot]
            ssem = ("xsa", "xsb", "xsc", "xsd")[slot]
            Ts = Hb[:, 2 + (j % 4), :]
            go = (j % 4) * 16
            if seq - 4 in stripe_ops:
                mk.wait("scalar", tsem, stripe_ops[seq - 4][0])
            for g in range(16):
                sl = mk.op("scalar", "s", 1, lambda e, g=g, go=go, slot=slot, Ts=Ts: e.activation(
                    xnb[:, slot, g * 64:(g + 1) * 64], Ts[:, g * 64:(g + 1) * 64], AF.Identity,
                    bias=gst[:, 6, go + g:go + g + 1], scale=gst[:, 5, go + g:go + g + 1]))
            slices_last[ps] = sl
            mk.wait("sync", "s", sl)
            if seq - 4 in stripe_ops:
                mk.wait("sync", "v", stripe_ops[seq - 4][2])
            trv = mk.op("sync", tsem, 16, lambda e, slot=slot: e.dma_start(
                xnT[:, slot], xnb[:, slot, :], transpose=True))
            if seq - 4 in stripe_ops:
                mk.wait("gpsimd", ssem, stripe_ops[seq - 4][1])
            sld = mk.op("gpsimd", lsem, 16, lambda e, ch=ch, j=j, slot=slot: e.dma_start(
                xstr[:, slot], xr[:, ch * 8:(ch + 1) * 8, j * 128:(j + 1) * 128]))
            mk.wait("vector", tsem, trv)
            mk.wait("vector", lsem, sld)
            for k in range(8):
                ct = ch * 8 + k
                mk.op("vector", "v", 1, lambda e, k=k, ct=ct, slot=slot: e.tensor_scalar(
                    xstr[:, slot, k, :], xstr[:, slot, k, :], 1.0, a["gnbT"][:, ct:ct + 1],
                    ALU.mult, ALU.add))
            mk.wait("vector", "v", mk.cnt["v"])
            for k in range(8):
                ct = ch * 8 + k
                av = mk.op("vector", "v", 1, lambda e, k=k, ct=ct, slot=slot: e.scalar_tensor_tensor(
                    xstr[:, slot, k, :], xnT[:, slot, k, :], a["gnwT"][:, ct:ct + 1],
                    xstr[:, slot, k, :], ALU.mult, ALU.add))
            mk.wait("gpsimd", "v", av)
            stv = mk.op("gpsimd", ssem, 16, lambda e, ch=ch, j=j, slot=slot: e.dma_start(
                xdr[:, ch * 8:(ch + 1) * 8, j * 128:(j + 1) * 128], xstr[:, slot]))
            stripe_ops[seq] = (trv, stv, av)
            state["last_store"] = stv

    for ch in range(2):
        mk.wait("gpsimd", "t", state["qmm"])
        wq_done = mk.op("gpsimd", "wq", 16, lambda e, ch=ch: e.dma_start(
            wq[:], WQ[:, ch * 1024:(ch + 1) * 1024].rearrange("(k p) c -> p k c", p=128)))
        mk.wait("tensor", "wq", wq_done)
        for j in range(NT):
            emitQ(ch, j)
            emitA(ch, j)
            if j % 2 == 1:
                emitB1(ch, j - 1)
                emitB1(ch, j)
                emitB2(ch, j)
    last_store = state["last_store"]
    return last_store



from concourse.bass_utils import run_bass_kernel_spmd

NCORES = 8
ROPE_DIM = 64
_CACHE = {}


def _rope_tables():
    inv_freq = (1.0 / 10000.0 ** (np.arange(0, ROPE_DIM, 2, dtype=np.float32) / ROPE_DIM)).astype(np.float32)
    freqs = np.arange(TT, dtype=np.float32)[:, None] * inv_freq[None, :]
    freqs = np.repeat(freqs, 2, axis=-1)
    return np.cos(freqs).astype(np.float32), np.sin(freqs).astype(np.float32)


def build_program():
    nc = bass.Bass("TRN2", target_bir_lowering=False)
    x0 = nc.dram_tensor("x0", [TT, D], F32, kind="ExternalInput").ap()
    W13t = nc.dram_tensor("w13", [NU, 2, NI, 128, ND, 128], F32, kind="ExternalInput").ap()
    W2t = nc.dram_tensor("w2", [NU, 4, 128, NI, 512], F32, kind="ExternalInput").ap()
    GWTt = nc.dram_tensor("gwt", [D, 16], F32, kind="ExternalInput").ap()
    EBt = nc.dram_tensor("eb", [128, 16], F32, kind="ExternalInput").ap()
    WQt = nc.dram_tensor("wqd", [D, D], F32, kind="ExternalInput").ap()
    SMAt = nc.dram_tensor("sma", [128, D + 32 + 2 * NT * 64], F32, kind="ExternalInput").ap()
    Ct = nc.dram_tensor("consts", [128, 2], F32, kind="ExternalInput").ap()
    xm1 = nc.dram_tensor("xm1", [TT, D], F32, kind="Internal").ap()
    xm2 = nc.dram_tensor("xm2", [TT, D], F32, kind="Internal").ap()
    xo = nc.dram_tensor("xo", [TT, D], F32, kind="ExternalOutput").ap()

    mk = Mk(nc)
    with ExitStack() as es:
        for s in ("xa", "xb", "xc", "xd", "xsa", "xsb", "xsc", "xsd", "lta", "ltb", "ltc", "ltd",
                  "lt4", "lt5", "lt6", "lt7", "xe", "xf", "xg", "xh", "wr0", "wr1", "wr2", "wb0", "wb1", "wb2", "pre", "wq", "g", "t", "v", "s"):
            mk.new_sem(s, es)
        gwt_s = es.enter_context(nc.sbuf_tensor("gwt_s", [128, ND, 16], BF16))
        eb_s = es.enter_context(nc.sbuf_tensor("eb_s", [128, 16], F32))
        cp = es.enter_context(nc.sbuf_tensor("cp", [128, 2], F32))
        p1 = es.enter_context(nc.psum_tensor("p1", [128, 2048], F32))
        p2 = es.enter_context(nc.psum_tensor("p2", [128, 2048], F32))
        nc.const_aps.register(F32, EPS, cp[:, 0:1])
        nc.const_aps.register(F32, -0.5, cp[:, 1:2])

        with ExitStack() as es1:
            t = alloc_ffn_tensors(nc, es1, "f1")
            mk.op("gpsimd", "pre", 16, lambda e: e.dma_start(
                gwt_s[:], GWTt.rearrange("(k p) e -> p k e", p=128)))
            mk.op("gpsimd", "pre", 16, lambda e: e.dma_start(eb_s[:], EBt))
            pre = mk.op("gpsimd", "pre", 16, lambda e: e.dma_start(cp[:], Ct))
            mk.wait("tensor", "pre", pre)
            mk.wait("vector", "pre", pre)
            mk.wait("scalar", "pre", pre)
            emit_ffn_call(mk, t, x0, xm1, W13t, W2t, gwt_s, eb_s, p1, p2)
            mk.run_block()

        with ExitStack() as es2:
            a = alloc_attn_tensors(nc, es2)
            mk.wait_stores("sync")
            mk.wait_stores("gpsimd")
            emit_attn(mk, a, xm1, xm2, WQt, SMAt, p1, p2)
            mk.run_block()

        with ExitStack() as es3:
            t2 = alloc_ffn_tensors(nc, es3, "f2")
            mk.wait_stores("sync")
            mk.wait_stores("gpsimd")
            emit_ffn_call(mk, t2, xm2, xo, W13t, W2t, gwt_s, eb_s, p1, p2)
            mk.wait_stores("sync")
            mk.run_block()
    return nc


def _prep_weights(inp):
    w1, w3, w2 = inp["L0_ffn_w1"], inp["L0_ffn_w3"], inp["L0_ffn_w2"]
    sw1, sw3, sw2 = inp["L0_ffn_sw1"], inp["L0_ffn_sw3"], inp["L0_ffn_sw2"]
    w1u = np.concatenate([w1, sw1.reshape(D, 2, TI).transpose(1, 0, 2)], 0)
    w3u = np.concatenate([w3, sw3.reshape(D, 2, TI).transpose(1, 0, 2)], 0)
    w2u = np.concatenate([w2, sw2.reshape(2, TI, D)], 0)
    W13 = np.ascontiguousarray(np.stack([
        w1u.reshape(NU, ND, 128, NI, 128).transpose(0, 3, 2, 1, 4),
        w3u.reshape(NU, ND, 128, NI, 128).transpose(0, 3, 2, 1, 4)], axis=1))
    W2 = np.ascontiguousarray(w2u.reshape(NU, NI, 128, 4, 512).transpose(0, 3, 2, 1, 4))
    GWT = np.ascontiguousarray(inp["L0_ffn_gate_w"].T)
    EB = np.ascontiguousarray(np.tile(inp["L0_ffn_expert_bias"][None, :], (128, 1)))
    WQ = np.ascontiguousarray(inp["L0_attn_W_Q"])
    cos, sin = _rope_tables()
    cosl = cos.reshape(NT, 128, 64).transpose(1, 0, 2).reshape(128, NT * 64)
    sinl = sin.reshape(NT, 128, 64).transpose(1, 0, 2).reshape(128, NT * 64)
    SMA = np.ascontiguousarray(np.concatenate([
        np.tile(inp["L0_attn_hh_b"][None, :], (128, 1)),
        inp["L0_attn_gn_weight"].reshape(16, 128).T,
        inp["L0_attn_gn_bias"].reshape(16, 128).T,
        cosl, sinl], axis=1)).astype(np.float32)
    CONSTS = np.tile(np.array([[EPS, -0.5]], np.float32), (128, 1))
    return W13, W2, GWT, EB, WQ, SMA, CONSTS


def kernel(**inp):
    _ = inp.get("L0_attn_S_n")
    X = inp["L0_attn_X"]
    W13, W2, GWT, EB, WQ, SMA, CONSTS = _prep_weights(inp)
    if "nc" not in _CACHE:
        _CACHE["nc"] = build_program()
    nc = _CACHE["nc"]
    in_maps = [dict(x0=np.ascontiguousarray(X[c]), w13=W13, w2=W2, gwt=GWT, eb=EB,
                    wqd=WQ, sma=SMA, consts=CONSTS) for c in range(NCORES)]
    res = run_bass_kernel_spmd(nc, in_maps, core_ids=list(range(NCORES)))
    return np.stack([np.asarray(res.results[c]["xo"], dtype=np.float32) for c in range(NCORES)], axis=0)


# version 08
# speedup vs baseline: 1.1319x; 1.0809x over previous
import numpy as np
from contextlib import ExitStack
import concourse.bass as bass
import concourse.mybir as mybir

F32 = mybir.dt.float32
I32 = mybir.dt.int32
BF16 = mybir.dt.bfloat16
AF = mybir.ActivationFunctionType
ALU = mybir.AluOpType
AX = mybir.AxisListType

D = 2048
TI = 1408
NU = 18
NI = 11
ND = 16
TT = 2048
TH = 1024
NJ = 8
EPS = 1e-5
BIG = 1e30
NEG_BIG = -1e30
RING13 = 3

ENGS = ("tensor", "vector", "scalar", "sync", "gpsimd")


class Mk:
    def __init__(self, nc):
        self.nc = nc
        self.prog = {e: [] for e in ENGS}
        self.cnt = {}
        self.sem = {}
        self.bank_free = {}
        self.bank_ready = {}
        self.emitted_wait = {}
        self.w13_slot_free = [0] * RING13
        self.w2_slot_free = [0] * 3
        self.sil_free = [[0, 0], [0, 0]]
        self.w13_seq = 0
        self.w13_seq2 = 0
        from collections import deque
        self.w13_jobs = deque()
        self.w2_jobs = deque()
        self.w13_loaded = {}
        self.w2_loaded = {}
        self.w2_emit_seq = 0
        self.w2_cons_seq = 0
        self.call_seq = 0

    def new_sem(self, name, es):
        self.sem[name] = es.enter_context(self.nc.semaphore(name))
        self.cnt[name] = 0

    def emit(self, eng, fn):
        self.prog[eng].append(fn)

    def wait(self, eng, name, val):
        if val <= 0:
            return
        key = (eng, name)
        if val <= self.emitted_wait.get(key, -1):
            return
        self.emitted_wait[key] = val
        s = self.sem[name]
        self.emit(eng, lambda e, s=s, v=val: e.wait_ge(s, v))

    def wait_bank_free(self, eng, bank):
        if bank in self.bank_free:
            nm, v = self.bank_free[bank]
            self.wait(eng, nm, v)

    def wait_bank_ready(self, eng, bank):
        nm, v = self.bank_ready[bank]
        self.wait(eng, nm, v)

    def op(self, eng, name, n, fn):
        self.cnt[name] += n
        v = self.cnt[name]
        s = self.sem[name]
        self.emit(eng, lambda e, fn=fn, s=s, n=n: fn(e).then_inc(s, n))
        return v

    def raw(self, eng, fn):
        self.emit(eng, fn)

    def wait_stores(self, eng):
        for nm in ("xsa", "xsb", "xsc", "xsd"):
            self.wait(eng, nm, self.cnt[nm])

    def chain(self, eng, name, fn):
        self.wait(eng, name, self.cnt[name])
        return self.op(eng, name, 1, fn)

    def run_block(self):
        nc = self.nc
        with nc.Block() as block:
            @block.tensor
            def _(t):
                for f in self.prog["tensor"]:
                    f(t)

            @block.vector
            def _(v):
                for f in self.prog["vector"]:
                    f(v)

            @block.scalar
            def _(s):
                for f in self.prog["scalar"]:
                    f(s)

            @block.sync
            def _(sy):
                for f in self.prog["sync"]:
                    f(sy)

            @block.gpsimd
            def _(g):
                for f in self.prog["gpsimd"]:
                    f(g)
        self.prog = {e: [] for e in ENGS}


def alloc_ffn_tensors(nc, es, pfx=""):
    def sb(name, shape, dtype, **kw):
        return es.enter_context(nc.sbuf_tensor(pfx + name, shape, dtype, **kw))
    t = {}
    t["x_h"] = sb("x_h", [128, NJ, D], F32)
    t["lnxT"] = sb("lnxT", [128, ND, TH], BF16, is_transpose_dest=True)
    t["hT"] = sb("hT", [128, 2, NI, TH], BF16)
    t["w13r"] = sb("w13r", [128, RING13, 2, ND, 128], BF16)
    t["w2r"] = sb("w2r", [128, NI, 512], BF16)
    t["lnbf"] = sb("lnbf", [128, 2, D], BF16)
    t["sil"] = sb("sil", [128, 2, 2, 512], F32)
    t["scr"] = sb("scr", [128, D], F32)
    t["stat"] = sb("stat", [128, 8, NJ], F32)
    t["sgate"] = sb("sgate", [128, NJ, 16], F32)
    t["cw05"] = sb("cw05", [128, NJ, NU], F32)
    t["gt"] = sb("gt", [128, 8, 16], F32)
    t["gsml"] = sb("gsml", [128, 8], F32)
    return t


def emit_x_load(mk, x_src_t, x_h, half, g):
    xr = x_src_t.rearrange("(j p) d -> p j d", p=128)
    j0 = half * NJ + g * 4
    sem = "xa" if g == 0 else "xb"
    v = mk.op("sync", sem, 16, lambda e, g=g, j0=j0: e.dma_start(
        x_h[:, g * 4:(g + 1) * 4, :], xr[:, j0:j0 + 4, :]))
    return sem, v


def emit_stats_tile(mk, t, x_h, j, xsem, xval):
    stat, scr = t["stat"], t["scr"]
    mk.wait("vector", xsem, xval)
    mk.wait("scalar", xsem, xval)
    mk.op("vector", "v", 1, lambda e, j=j: e.reduce_sum(
        stat[:, 0, j:j + 1], x_h[:, j, :], axis=AX.X))
    sv = mk.chain("scalar", "s", lambda e, j=j: e.activation(
        scr[:], x_h[:, j, :], AF.Square, accum_out=stat[:, 1, j:j + 1]))
    return sv


def emit_ln_planes(mk, t, last_sq):
    stat = t["stat"]
    mk.wait("vector", "s", last_sq)
    mk.chain("vector", "v", lambda e: e.tensor_scalar(
        stat[:, 2, :], stat[:, 0, :], 1.0 / D, None, ALU.mult))
    mk.chain("vector", "v", lambda e: e.tensor_scalar(
        stat[:, 3, :], stat[:, 1, :], 1.0 / D, None, ALU.mult))
    mk.chain("vector", "v", lambda e: e.tensor_tensor(
        stat[:, 4, :], stat[:, 2, :], stat[:, 2, :], ALU.mult))
    vv = mk.chain("vector", "v", lambda e: e.tensor_sub(
        stat[:, 4, :], stat[:, 3, :], stat[:, 4, :]))
    mk.wait("scalar", "v", vv)
    mk.chain("scalar", "s", lambda e: e.activation(
        stat[:, 7, :], stat[:, 4, :], AF.Ln, bias=EPS))
    sv = mk.chain("scalar", "s", lambda e: e.activation(
        stat[:, 5, :], stat[:, 7, :], AF.Exp, scale=-0.5))
    mk.wait("vector", "s", sv)
    mk.chain("vector", "v", lambda e: e.tensor_tensor(
        stat[:, 6, :], stat[:, 2, :], stat[:, 5, :], ALU.mult))
    mk.chain("vector", "v", lambda e: e.tensor_scalar(
        stat[:, 6, :], stat[:, 6, :], -1.0, None, ALU.mult))


def emit_norm_tr(mk, t, lnxT, x_h):
    stat, lnbf = t["stat"], t["lnbf"]
    tr_done = []
    for j in range(NJ):
        par = j % 2
        lsem = "lta" if par == 0 else "ltb"
        if j >= 2:
            mk.wait("vector", lsem, tr_done[j - 2])
        emitfn = mk.chain if j == 0 else (lambda eng, nm, f: mk.op(eng, nm, 1, f))
        vv = emitfn("vector", "v", lambda e, j=j, par=par: e.tensor_scalar(
            lnbf[:, par, :], x_h[:, j, :], stat[:, 5, j:j + 1], stat[:, 6, j:j + 1],
            ALU.mult, ALU.add))
        mk.wait("sync", "v", vv)
        tv = mk.op("sync", lsem, 16, lambda e, j=j, par=par: e.dma_start(
            lnxT[:, :, j * 128:(j + 1) * 128], lnbf[:, par, :], transpose=True))
        tr_done.append(tv)
    return tr_done


def emit_ln_pass(mk, t, x_src_t, half, lnxT, x_h):
    mk.wait_stores("sync")
    loads = [emit_x_load(mk, x_src_t, x_h, half, g) for g in range(2)]
    sv = 0
    for j in range(NJ):
        sem, val = loads[j // 4]
        sv = emit_stats_tile(mk, t, x_h, j, sem, val)
    emit_ln_planes(mk, t, sv)
    return emit_norm_tr(mk, t, lnxT, x_h)


def emit_gate(mk, t, gwt_s, eb_s, p2, lnxT_tr_done):
    lnxT, sgate, cw05, gt, gsml = t["lnxT"], t["sgate"], t["cw05"], t["gt"], t["gsml"]
    mk.wait("tensor", "lta", mk.cnt["lta"])
    mk.wait("tensor", "ltb", mk.cnt["ltb"])
    sig_done = [0, 0]
    gate_mm_done = [0, 0]
    for j in range(NJ):
        q = j % 2
        pg = p2[:, q * 512:q * 512 + 16]
        if j >= 2:
            mk.wait("tensor", "s", sig_done[q])
        else:
            mk.wait_bank_free("tensor", ("p2", q))
        for d in range(ND):
            if d < ND - 1:
                mk.raw("tensor", lambda e, d=d, j=j, pg=pg: e.matmul(
                    pg, lnxT[:, d, j * 128:(j + 1) * 128], gwt_s[:, d, :],
                    start=(d == 0), stop=False))
            else:
                gate_mm_done[q] = mk.op("tensor", "t", 1, lambda e, d=d, j=j, pg=pg: e.matmul(
                    pg, lnxT[:, d, j * 128:(j + 1) * 128], gwt_s[:, d, :],
                    start=False, stop=True))
        mk.wait("scalar", "t", gate_mm_done[q])
        sig_done[q] = mk.op("scalar", "s", 1, lambda e, j=j, pg=pg: e.activation(
            sgate[:, j, :], pg, AF.Sigmoid))

        mk.wait("vector", "s", sig_done[q])
        mk.chain("vector", "v", lambda e, j=j: e.tensor_tensor(
            gt[:, 0, :], sgate[:, j, :], eb_s[:], ALU.add))
        mk.chain("vector", "v", lambda e: e.reduce_max(
            gsml[:, 0:4], gt[:, 0, :].rearrange("p (a b) -> p a b", a=4), axis=AX.X))
        mk.chain("vector", "v", lambda e: e.reduce_max(
            gsml[:, 4:5], gsml[:, 0:4], axis=AX.X))
        mk.chain("vector", "v", lambda e: e.tensor_scalar(
            gt[:, 7, 0:4], gsml[:, 0:4], gsml[:, 4:5], None, ALU.is_ge))
        mk.chain("vector", "v", lambda e: e.scalar_tensor_tensor(
            gt[:, 7, 4:8], gt[:, 7, 0:4], NEG_BIG, gsml[:, 0:4], ALU.mult, ALU.add))
        mk.chain("vector", "v", lambda e: e.reduce_max(
            gsml[:, 4:5], gt[:, 7, 4:8], axis=AX.X))
        mk.chain("vector", "v", lambda e: e.tensor_scalar(
            gt[:, 7, 0:4], gsml[:, 0:4], gsml[:, 4:5], None, ALU.is_ge))
        mk.chain("vector", "v", lambda e: e.tensor_copy(
            gt[:, 1, :].rearrange("p (a b) -> p a b", a=4),
            gt[:, 7, 0:4].unsqueeze(2).broadcast_to((128, 4, 4))))
        mk.chain("vector", "v", lambda e: e.tensor_tensor(
            gt[:, 2, :], gt[:, 0, :], gt[:, 1, :], ALU.mult))
        mk.chain("vector", "v", lambda e: e.tensor_scalar(
            gt[:, 3, :], gt[:, 1, :], 1.0, None, ALU.subtract))
        mk.chain("vector", "v", lambda e: e.scalar_tensor_tensor(
            gt[:, 3, :], gt[:, 3, :], BIG, gt[:, 2, :], ALU.mult, ALU.add))
        mk.chain("vector", "v", lambda e: e.tensor_copy(gt[:, 4, :], gt[:, 3, :]))
        for k in range(3):
            a = 4 + (k % 2)
            b = 4 + ((k + 1) % 2)
            mk.chain("vector", "v", lambda e, a=a: e.reduce_max(
                gsml[:, 4:5], gt[:, a, :], axis=AX.X))
            mk.chain("vector", "v", lambda e, a=a: e.tensor_scalar(
                gt[:, 6, :], gt[:, a, :], gsml[:, 4:5], None, ALU.is_ge))
            mk.chain("vector", "v", lambda e, a=a, b=b: e.scalar_tensor_tensor(
                gt[:, b, :], gt[:, 6, :], NEG_BIG, gt[:, a, :], ALU.mult, ALU.add))
        mk.chain("vector", "v", lambda e: e.reduce_max(
            gsml[:, 4:5], gt[:, 5, :], axis=AX.X))
        mk.chain("vector", "v", lambda e: e.tensor_scalar(
            gt[:, 6, :], gt[:, 3, :], gsml[:, 4:5], None, ALU.is_ge))
        mk.chain("vector", "v", lambda e, j=j: e.tensor_tensor(
            gt[:, 6, :], sgate[:, j, :], gt[:, 6, :], ALU.mult))
        mk.chain("vector", "v", lambda e: e.reduce_sum(
            gsml[:, 5:6], gt[:, 6, :], axis=AX.X))
        mk.chain("vector", "v", lambda e: e.reciprocal(gsml[:, 6:7], gsml[:, 5:6]))
        mk.chain("vector", "v", lambda e: e.tensor_scalar(
            gsml[:, 6:7], gsml[:, 6:7], 0.5, None, ALU.mult))
        mk.chain("vector", "v", lambda e, j=j: e.tensor_scalar(
            cw05[:, j, 0:16], gt[:, 6, :], gsml[:, 6:7], None, ALU.mult))
    gv = mk.op("vector", "v", 1, lambda e: e.memset(cw05[:, :, 16:18], 0.5))
    mk.bank_free[("p2", 0)] = ("s", sig_done[0])
    mk.bank_free[("p2", 1)] = ("s", sig_done[1])
    return gv


def emit_ffn_units(mk, t, W13, W2, p1, p2, half, cw_ready, x_dst_t, jg_hook=None):
    x_h, lnxT, hT, w13r, w2r, sil, cw05 = (
        t["x_h"], t["lnxT"], t["hT"], t["w13r"], t["w2r"], t["sil"], t["cw05"])

    W13r = [[W13[u, m] for m in range(2)] for u in range(NU)]

    mk.call_seq += 1
    cs = mk.call_seq
    for u in range(NU):
        for i in range(NI):
            mk.w13_jobs.append((cs, u, i, W13r))
        for dq in range(4):
            for bi, (s0, bn) in enumerate(((0, 4), (4, 4), (8, 3))):
                mk.w2_jobs.append((cs, u, dq, bi, s0, bn, W2))

    def pump_w13():
        if not mk.w13_jobs:
            return
        jcs, u, i, Wr = mk.w13_jobs.popleft()
        slot = mk.w13_seq % RING13
        mk.w13_seq += 1
        mk.wait("gpsimd", "t", mk.w13_slot_free[slot])
        for m in range(2):
            v = mk.op("gpsimd", "wr%d" % slot, 16, lambda e, u=u, m=m, i=i, slot=slot, Wr=Wr: e.dma_start(
                w13r[:, slot, m], Wr[u][m][i]))
        mk.w13_loaded[(jcs, u, i)] = (slot, v)

    def pump_w2():
        if not mk.w2_jobs:
            return
        jcs, u, dq, bi, s0, bn, W2a = mk.w2_jobs.popleft()
        mk.wait("gpsimd", "t", mk.w2_slot_free[bi])
        v = mk.op("gpsimd", "wb%d" % bi, 16, lambda e, u=u, dq=dq, s0=s0, bn=bn, W2a=W2a: e.dma_start(
            w2r[:, s0:s0 + bn, :], W2a[u, dq, :, s0:s0 + bn, :]))
        mk.w2_loaded[(jcs, u, dq, bi)] = v
        mk.w2_emit_seq += 1

    while mk.w13_seq < mk.w13_seq2 + RING13 and mk.w13_jobs:
        pump_w13()
    while mk.w2_emit_seq < mk.w2_cons_seq + 3 and mk.w2_jobs:
        pump_w2()

    hT_ready = {}
    sil_free = mk.sil_free
    unit_final_stt = {}
    cw_waited = [False]

    def emit_p1_block(u, i):
        ubuf = u % 2
        slot = mk.w13_seq2 % RING13
        mk.w13_seq2 += 1
        _wslot, _wval = mk.w13_loaded[(cs, u, i)]
        mk.wait("tensor", "wr%d" % _wslot, _wval)
        par = i % 2
        last_mm = 0
        for m in range(2):
            for t2 in range(2):
                bank = ("p1", m * 2 + t2)
                mk.wait_bank_free("tensor", bank)
                for d in range(ND):
                    if d < ND - 1:
                        mk.raw("tensor", lambda e, slot=slot, m=m, t2=t2, d=d: e.matmul(
                            p1[:, (m * 2 + t2) * 512:(m * 2 + t2 + 1) * 512],
                            w13r[:, slot, m, d, :], lnxT[:, d, t2 * 512:(t2 + 1) * 512],
                            start=(d == 0), stop=False))
                    else:
                        last_mm = mk.op("tensor", "t", 1, lambda e, slot=slot, m=m, t2=t2, d=d: e.matmul(
                            p1[:, (m * 2 + t2) * 512:(m * 2 + t2 + 1) * 512],
                            w13r[:, slot, m, d, :], lnxT[:, d, t2 * 512:(t2 + 1) * 512],
                            start=False, stop=True))
                mk.bank_ready[bank] = ("t", last_mm)
        mk.w13_slot_free[slot] = last_mm
        pump_w13()
        for t2 in range(2):
            mk.wait_bank_ready("scalar", ("p1", t2))
            mk.wait("scalar", "v", sil_free[par][t2])
            sv = mk.op("scalar", "s", 1, lambda e, par=par, t2=t2: e.activation(
                sil[:, par, t2, :], p1[:, t2 * 512:(t2 + 1) * 512], AF.Silu))
            mk.bank_free[("p1", t2)] = ("s", sv)
            mk.wait_bank_ready("vector", ("p1", 2 + t2))
            mk.wait("vector", "s", sv)
            vv = mk.op("vector", "v", 1, lambda e, ubuf=ubuf, i=i, par=par, t2=t2: e.tensor_tensor(
                hT[:, ubuf, i, t2 * 512:(t2 + 1) * 512], sil[:, par, t2, :],
                p1[:, (2 + t2) * 512:(3 + t2) * 512], ALU.mult))
            mk.bank_free[("p1", 2 + t2)] = ("v", vv)
            sil_free[par][t2] = vv
        hT_ready[(u, i)] = vv

    def emit_p2_block(u, dq, jg):
        ubuf = u % 2
        if not cw_waited[0]:
            mk.wait("vector", "v", cw_ready)
            cw_waited[0] = True
        last_mm = 0
        for jj in range(4):
            mk.wait_bank_free("tensor", ("p2", jj))
        for i in range(NI):
            bi = 0 if i < 4 else (1 if i < 8 else 2)
            mk.wait("tensor", "wb%d" % bi, mk.w2_loaded[(cs, u, dq, bi)])
            if dq == 0 and jg == 0:
                mk.wait("tensor", "v", hT_ready[(u, i)])
            boundary = (jg == 1 and i in (3, 7, 10))
            for jj in range(4):
                j = jg * 4 + jj
                def _mm(e, ubuf=ubuf, i=i, j=j, jj=jj):
                    return e.matmul(
                        p2[:, jj * 512:(jj + 1) * 512],
                        hT[:, ubuf, i, j * 128:(j + 1) * 128], w2r[:, i, :],
                        start=(i == 0), stop=(i == NI - 1))
                if jj == 3 and (boundary or i == NI - 1):
                    lm = mk.op("tensor", "t", 1, _mm)
                    if i == NI - 1:
                        last_mm = lm
                else:
                    mk.raw("tensor", _mm)
            if boundary:
                mk.w2_slot_free[{3: 0, 7: 1, 10: 2}[i]] = lm
                mk.w2_cons_seq += 1
                pump_w2()
        for jj in range(4):
            j = jg * 4 + jj
            mk.wait("vector", "t", last_mm)
            vv = mk.op("vector", "v", 1, lambda e, jj=jj, j=j, dq=dq, u=u: e.scalar_tensor_tensor(
                x_h[:, j, dq * 512:(dq + 1) * 512], p2[:, jj * 512:(jj + 1) * 512],
                cw05[:, j, u:u + 1], x_h[:, j, dq * 512:(dq + 1) * 512],
                ALU.mult, ALU.add))
            mk.bank_free[("p2", jj)] = ("v", vv)
            if u == NU - 1 and dq == 3:
                unit_final_stt[jg] = vv
        if u == NU - 1 and dq == 3 and jg_hook is not None:
            jg_hook(jg, unit_final_stt[jg])

    for i in range(NI):
        emit_p1_block(0, i)
    for u in range(NU):
        p2_blocks = [(dq, jg) for dq in range(4) for jg in range(2)]
        p1_blocks = list(range(NI)) if u + 1 < NU else []
        n2 = 0
        for k, i in enumerate(p1_blocks):
            while n2 < len(p2_blocks) and n2 < (k * 8) // 10 + 1:
                emit_p2_block(u, *p2_blocks[n2])
                n2 += 1
            emit_p1_block(u + 1, i)
        while n2 < len(p2_blocks):
            emit_p2_block(u, *p2_blocks[n2])
            n2 += 1

    store_done = []
    if jg_hook is None:
        xdr = x_dst_t.rearrange("(j p) d -> p j d", p=128)
        for jg in range(2):
            sem = "xsa" if jg == 0 else "xsb"
            mk.wait("sync", "v", unit_final_stt[jg])
            v = mk.op("sync", sem, 16, lambda e, jg=jg, half=half: e.dma_start(
                xdr[:, half * NJ + jg * 4: half * NJ + jg * 4 + 4, :],
                x_h[:, jg * 4:(jg + 1) * 4, :]))
            store_done.append(v)
    return store_done


def emit_ffn_call(mk, t, x_src_t, x_dst_t, W13, W2, gwt_s, eb_s, p1, p2):
    xdr = x_dst_t.rearrange("(j p) d -> p j d", p=128)
    x_h = t["x_h"]
    stats_last = [0]

    def make_hook(half):
        def hook(jg, final_stt):
            sem = "xsa" if jg == 0 else "xsb"
            mk.wait("sync", "v", final_stt)
            stv = mk.op("sync", sem, 16, lambda e, jg=jg, half=half: e.dma_start(
                xdr[:, half * NJ + jg * 4: half * NJ + jg * 4 + 4, :],
                x_h[:, jg * 4:(jg + 1) * 4, :]))
            if half == 1:
                return
            mk.wait("sync", sem, stv)
            lsem, lval = emit_x_load(mk, x_src_t, x_h, 1, jg)
            for j in range(jg * 4, jg * 4 + 4):
                stats_last[0] = emit_stats_tile(mk, t, x_h, j, lsem, lval)
        return hook

    tr_done = emit_ln_pass(mk, t, x_src_t, 0, t["lnxT"], x_h)
    cw_ready = emit_gate(mk, t, gwt_s, eb_s, p2, tr_done)
    emit_ffn_units(mk, t, W13, W2, p1, p2, 0, cw_ready, x_dst_t, jg_hook=make_hook(0))
    emit_ln_planes(mk, t, stats_last[0])
    tr_done = emit_norm_tr(mk, t, t["lnxT"], x_h)
    cw_ready = emit_gate(mk, t, gwt_s, eb_s, p2, tr_done)
    st = emit_ffn_units(mk, t, W13, W2, p1, p2, 1, cw_ready, x_dst_t, jg_hook=make_hook(1))
    return st



NT = 16


def alloc_attn_tensors(nc, es):
    a = {}
    a["ln2T"] = es.enter_context(nc.sbuf_tensor("ln2T", [128, ND, TT], BF16, is_transpose_dest=True))
    a["wq"] = es.enter_context(nc.sbuf_tensor("wq", [128, ND, 1024], BF16))
    a["u4"] = es.enter_context(nc.sbuf_tensor("u4", [128, 6144], F32))
    a["lnbf2"] = es.enter_context(nc.sbuf_tensor("lnbf2", [128, 2, D], BF16))
    a["stat2"] = es.enter_context(nc.sbuf_tensor("stat2", [128, 8, NT], F32))
    a["scr2"] = es.enter_context(nc.sbuf_tensor("scr2", [128, 4, 1024], F32))
    a["xnb"] = es.enter_context(nc.sbuf_tensor("xnb", [128, 4, 1024], BF16))
    a["xnT"] = es.enter_context(nc.sbuf_tensor("xnT", [128, 4, 8, 128], BF16, is_transpose_dest=True))
    a["xstr"] = es.enter_context(nc.sbuf_tensor("xstr", [128, 4, 8, 128], F32))
    a["gst"] = es.enter_context(nc.sbuf_tensor("gst", [128, 8, 64], F32))
    a["rtmp"] = es.enter_context(nc.sbuf_tensor("rtmp", [128, 3, 64], F32))
    a["hhb"] = es.enter_context(nc.sbuf_tensor("hhb", [128, D], F32))
    a["gnwT"] = es.enter_context(nc.sbuf_tensor("gnwT", [128, 16], F32))
    a["gnbT"] = es.enter_context(nc.sbuf_tensor("gnbT", [128, 16], F32))
    a["cosb"] = es.enter_context(nc.sbuf_tensor("cosb", [128, NT, 64], F32))
    a["sinb"] = es.enter_context(nc.sbuf_tensor("sinb", [128, NT, 64], F32))
    return a


def emit_attn(mk, a, x_src_t, x_dst_t, WQ, SMA, p1, p2):
    ln2T, wq, lnbf2, stat2, scr2 = a["ln2T"], a["wq"], a["lnbf2"], a["stat2"], a["scr2"]
    xnb, xnT, xstr, gst, rtmp = a["xnb"], a["xnT"], a["xstr"], a["gst"], a["rtmp"]
    xtmp = a["u4"][:, 0:6144].rearrange("p (r d) -> p r d", r=3)
    Hb = a["u4"][:, 0:6144].rearrange("p (r d) -> p r d", r=6)

    sm_done = mk.op("gpsimd", "pre", 16, lambda e: e.dma_start(a["hhb"][:], SMA[:, 0:D]))
    sm_done = mk.op("gpsimd", "pre", 16, lambda e: e.dma_start(a["gnwT"][:], SMA[:, D:D + 16]))
    sm_done = mk.op("gpsimd", "pre", 16, lambda e: e.dma_start(a["gnbT"][:], SMA[:, D + 16:D + 32]))
    sm_done = mk.op("gpsimd", "pre", 16, lambda e: e.dma_start(
        a["cosb"][:], SMA[:, D + 32:D + 32 + NT * 64].rearrange("p (j c) -> p j c", c=64)))
    sm_done = mk.op("gpsimd", "pre", 16, lambda e: e.dma_start(
        a["sinb"][:], SMA[:, D + 32 + NT * 64:D + 32 + 2 * NT * 64].rearrange("p (j c) -> p j c", c=64)))

    xr = x_src_t.rearrange("(j p) d -> p j d", p=128)
    xdr = x_dst_t.rearrange("(j p) d -> p j d", p=128)

    MAGIC = 0x5F3759DF
    norm_done = [0] * NT
    redv_done = [0] * NT
    reds_done = [0] * NT
    tr_done = [0] * NT
    for j in range(NT):
        slot = j % 3
        xsem = ("xa", "xb", "xc")[slot]
        lsem = "lta" if j % 2 == 0 else "ltb"
        if j >= 3:
            mk.wait("sync", "v", norm_done[j - 3])
        ld = mk.op("sync", xsem, 16, lambda e, j=j, slot=slot: e.dma_start(
            xtmp[:, slot, :], xr[:, j, :]))
        mk.wait("vector", xsem, ld)
        mk.wait("scalar", xsem, ld)
        redv_done[j] = mk.op("vector", "v", 1, lambda e, j=j, slot=slot: e.reduce_sum(
            stat2[:, 0, j:j + 1], xtmp[:, slot, :], axis=AX.X))
        sv = mk.chain("scalar", "s", lambda e, j=j, slot=slot: e.activation(
            scr2[:, 0, :], xtmp[:, slot, 0:1024], AF.Square, accum_out=stat2[:, 2, j:j + 1]))
        sv = mk.chain("scalar", "s", lambda e, j=j, slot=slot: e.activation(
            scr2[:, 0, :], xtmp[:, slot, 1024:2048], AF.Square, accum_out=stat2[:, 3, j:j + 1]))
        reds_done[j] = sv
        if j % 2 == 0:
            continue
        jj = j - 1
        mk.wait("vector", "s", sv)
        mk.chain("vector", "v", lambda e, jj=jj: e.tensor_add(
            stat2[:, 1, jj:jj + 2], stat2[:, 2, jj:jj + 2], stat2[:, 3, jj:jj + 2]))
        mk.chain("vector", "v", lambda e, jj=jj: e.tensor_scalar(
            stat2[:, 2:4:1, jj:jj + 2] if False else stat2[:, 2, jj:jj + 2],
            stat2[:, 0, jj:jj + 2], 1.0 / D, None, ALU.mult))
        mk.chain("vector", "v", lambda e, jj=jj: e.tensor_scalar(
            stat2[:, 3, jj:jj + 2], stat2[:, 1, jj:jj + 2], 1.0 / D, None, ALU.mult))
        mk.chain("vector", "v", lambda e, jj=jj: e.tensor_tensor(
            stat2[:, 4, jj:jj + 2], stat2[:, 2, jj:jj + 2], stat2[:, 2, jj:jj + 2], ALU.mult))
        mk.chain("vector", "v", lambda e, jj=jj: e.tensor_sub(
            stat2[:, 4, jj:jj + 2], stat2[:, 3, jj:jj + 2], stat2[:, 4, jj:jj + 2]))
        mk.chain("vector", "v", lambda e, jj=jj: e.tensor_scalar(
            stat2[:, 7, jj:jj + 2], stat2[:, 4, jj:jj + 2], EPS, None, ALU.add))
        mk.chain("vector", "v", lambda e, jj=jj: e.tensor_scalar(
            stat2[:, 5, jj:jj + 2].bitcast(I32), stat2[:, 7, jj:jj + 2].bitcast(I32), 1, None,
            ALU.logical_shift_right))
        mk.chain("vector", "v", lambda e, jj=jj: e.tensor_scalar(
            stat2[:, 5, jj:jj + 2].bitcast(I32), stat2[:, 5, jj:jj + 2].bitcast(I32), -1, MAGIC,
            ALU.mult, ALU.add))
        for _ in range(3):
            mk.chain("vector", "v", lambda e, jj=jj: e.tensor_tensor(
                stat2[:, 6, jj:jj + 2], stat2[:, 5, jj:jj + 2], stat2[:, 5, jj:jj + 2], ALU.mult))
            mk.chain("vector", "v", lambda e, jj=jj: e.tensor_tensor(
                stat2[:, 6, jj:jj + 2], stat2[:, 6, jj:jj + 2], stat2[:, 7, jj:jj + 2], ALU.mult))
            mk.chain("vector", "v", lambda e, jj=jj: e.tensor_scalar(
                stat2[:, 6, jj:jj + 2], stat2[:, 6, jj:jj + 2], -0.5, 1.5, ALU.mult, ALU.add))
            mk.chain("vector", "v", lambda e, jj=jj: e.tensor_tensor(
                stat2[:, 5, jj:jj + 2], stat2[:, 5, jj:jj + 2], stat2[:, 6, jj:jj + 2], ALU.mult))
        mk.chain("vector", "v", lambda e, jj=jj: e.scalar_tensor_tensor(
            stat2[:, 6, jj:jj + 2], stat2[:, 2, jj:jj + 2], -1.0, stat2[:, 5, jj:jj + 2],
            ALU.mult, ALU.mult))
        for jn in (jj, j):
            nslot = jn % 3
            nlsem = "lta" if jn % 2 == 0 else "ltb"
            npar = jn % 2
            if jn >= 2:
                mk.wait("vector", nlsem, tr_done[jn - 2])
            norm_done[jn] = mk.chain("vector", "v", lambda e, jn=jn, nslot=nslot, npar=npar: e.tensor_scalar(
                lnbf2[:, npar, :], xtmp[:, nslot, :], stat2[:, 5, jn:jn + 1], stat2[:, 6, jn:jn + 1],
                ALU.mult, ALU.add))
            mk.wait("sync", "v", norm_done[jn])
            tr_done[jn] = mk.op("sync", nlsem, 16, lambda e, jn=jn, npar=npar: e.dma_start(
                ln2T[:, :, jn * 128:(jn + 1) * 128], lnbf2[:, npar, :], transpose=True))

    mk.wait("vector", "pre", sm_done)
    mk.wait("gpsimd", "pre", sm_done)
    stripe_ops = {}
    tanh_done = [0, 0]
    gsq_done = [0, 0, 0, 0]
    slices_last = [0, 0]
    qmm_of = {}
    relu_of = {}
    state = {"last_store": 0, "qmm": 0}

    def emitQ(ch, j):
        par = j % 2
        pset = p1 if (j % 4) < 2 else p2
        pq = pset[:, par * 1024:(par + 1) * 1024]
        mk.wait("tensor", "lta" if par == 0 else "ltb", tr_done[j])
        mk.wait_bank_free("tensor", ("pq", j % 4))
        for d in range(ND):
            for c2 in range(2):
                fn = lambda e, d=d, c2=c2, j=j, pq=pq: e.matmul(
                    pq[:, c2 * 512:(c2 + 1) * 512], ln2T[:, d, j * 128:(j + 1) * 128],
                    wq[:, d, c2 * 512:(c2 + 1) * 512],
                    start=(d == 0), stop=(d == ND - 1))
                if d == ND - 1 and c2 == 1:
                    state["qmm"] = mk.op("tensor", "t", 1, fn)
                else:
                    mk.raw("tensor", fn)
        qmm_of[(ch, j)] = state["qmm"]
        mk.bank_ready[("pq", j % 4)] = ("t", state["qmm"])

    def emitA(ch, j):
        par = j % 2
        pset = p1 if (j % 4) < 2 else p2
        pq = pset[:, par * 1024:(par + 1) * 1024]
        H = Hb[:, par, :]
        mk.wait("vector", "t", qmm_of[(ch, j)])
        if ch == 0:
            mk.chain("vector", "v", lambda e, pq=pq: e.tensor_scalar(
                rtmp[:, 0, 0:64:2], pq[:, 1:64:2], -1.0, None, ALU.mult))
            mk.chain("vector", "v", lambda e, pq=pq: e.tensor_copy(
                rtmp[:, 0, 1:64:2], pq[:, 0:64:2]))
            mk.chain("vector", "v", lambda e, j=j: e.tensor_tensor(
                rtmp[:, 1, :], rtmp[:, 0, :], a["sinb"][:, j, :], ALU.mult))
            mk.chain("vector", "v", lambda e, pq=pq, j=j: e.tensor_tensor(
                rtmp[:, 2, :], pq[:, 0:64], a["cosb"][:, j, :], ALU.mult))
            mk.chain("vector", "v", lambda e, pq=pq: e.tensor_add(
                pq[:, 0:64], rtmp[:, 1, :], rtmp[:, 2, :]))
        mk.wait("vector", "s", tanh_done[par])
        rd = mk.chain("vector", "v", lambda e, pq=pq, ch=ch, H=H: e.scalar_tensor_tensor(
            H, pq[:], 0.0, a["hhb"][:, ch * 1024:(ch + 1) * 1024], ALU.max, ALU.add))
        mk.bank_free[("pq", j % 4)] = ("v", rd)
        relu_of[(ch, j)] = rd

    def emitB1(ch, j):
        par = j % 2
        q4 = j % 4
        H = Hb[:, par, :]
        Ts = Hb[:, 2 + q4, :]
        go = (j % 4) * 16
        mk.wait("scalar", "v", relu_of[(ch, j)])
        mk.wait("scalar", "v", gsq_done[q4])
        tv = mk.chain("scalar", "s", lambda e, H=H, Ts=Ts: e.activation(Ts, H, AF.Tanh))
        tanh_done[par] = tv
        sq = mk.chain("scalar", "s", lambda e, Ts=Ts, q4=q4: e.activation(
            scr2[:, q4, :], Ts, AF.Square))
        mk.wait("vector", "s", sq)
        mk.chain("vector", "v", lambda e, Ts=Ts, go=go: e.reduce_sum(
            gst[:, 0, go:go + 16], Ts.rearrange("p (g c) -> p g c", c=64), axis=AX.X))
        gq = mk.chain("vector", "v", lambda e, q4=q4, go=go: e.reduce_sum(
            gst[:, 1, go:go + 16], scr2[:, q4, :].rearrange("p (g c) -> p g c", c=64), axis=AX.X))
        gsq_done[q4] = gq

    MAGIC = 0x5F3759DF

    def emitB2(ch, jodd):
        ps = ((jodd // 2) % 2)
        s0 = ps * 32
        sl32 = slice(s0, s0 + 32)
        mk.wait("vector", "s", slices_last[ps])
        mk.chain("vector", "v", lambda e: e.tensor_scalar(
            gst[:, 2:4, sl32], gst[:, 0:2, sl32], 1.0 / 64, None, ALU.mult))
        mk.chain("vector", "v", lambda e: e.tensor_tensor(
            gst[:, 4, sl32], gst[:, 2, sl32], gst[:, 2, sl32], ALU.mult))
        mk.chain("vector", "v", lambda e: e.tensor_sub(
            gst[:, 4, sl32], gst[:, 3, sl32], gst[:, 4, sl32]))
        mk.chain("vector", "v", lambda e: e.tensor_scalar(
            gst[:, 7, sl32], gst[:, 4, sl32], EPS, None, ALU.add))
        mk.chain("vector", "v", lambda e: e.tensor_scalar(
            gst[:, 5, sl32].bitcast(I32), gst[:, 7, sl32].bitcast(I32), 1, None,
            ALU.logical_shift_right))
        mk.chain("vector", "v", lambda e: e.tensor_scalar(
            gst[:, 5, sl32].bitcast(I32), gst[:, 5, sl32].bitcast(I32), -1, MAGIC,
            ALU.mult, ALU.add))
        for _ in range(3):
            mk.chain("vector", "v", lambda e: e.tensor_tensor(
                gst[:, 3, sl32], gst[:, 5, sl32], gst[:, 5, sl32], ALU.mult))
            mk.chain("vector", "v", lambda e: e.tensor_tensor(
                gst[:, 3, sl32], gst[:, 3, sl32], gst[:, 7, sl32], ALU.mult))
            mk.chain("vector", "v", lambda e: e.tensor_scalar(
                gst[:, 3, sl32], gst[:, 3, sl32], -0.5, 1.5, ALU.mult, ALU.add))
            mk.chain("vector", "v", lambda e: e.tensor_tensor(
                gst[:, 5, sl32], gst[:, 5, sl32], gst[:, 3, sl32], ALU.mult))
        nm = mk.chain("vector", "v", lambda e: e.scalar_tensor_tensor(
            gst[:, 6, sl32], gst[:, 2, sl32], -1.0, gst[:, 5, sl32], ALU.mult, ALU.mult))
        mk.wait("scalar", "v", nm)
        for j in (jodd - 1, jodd):
            seq = ch * NT + j
            par = j % 2
            slot = seq % 4
            tsem = ("lt4", "lt5", "lt6", "lt7")[slot]
            lsem = ("xe", "xf", "xg", "xh")[slot]
            ssem = ("xsa", "xsb", "xsc", "xsd")[slot]
            Ts = Hb[:, 2 + (j % 4), :]
            go = (j % 4) * 16
            if seq - 4 in stripe_ops:
                mk.wait("scalar", tsem, stripe_ops[seq - 4][0])
            for g in range(16):
                sl = mk.op("scalar", "s", 1, lambda e, g=g, go=go, slot=slot, Ts=Ts: e.activation(
                    xnb[:, slot, g * 64:(g + 1) * 64], Ts[:, g * 64:(g + 1) * 64], AF.Identity,
                    bias=gst[:, 6, go + g:go + g + 1], scale=gst[:, 5, go + g:go + g + 1]))
            slices_last[ps] = sl
            mk.wait("sync", "s", sl)
            if seq - 4 in stripe_ops:
                mk.wait("sync", "v", stripe_ops[seq - 4][2])
            trv = mk.op("sync", tsem, 16, lambda e, slot=slot: e.dma_start(
                xnT[:, slot], xnb[:, slot, :], transpose=True))
            if seq - 4 in stripe_ops:
                mk.wait("gpsimd", ssem, stripe_ops[seq - 4][1])
            sld = mk.op("gpsimd", lsem, 16, lambda e, ch=ch, j=j, slot=slot: e.dma_start(
                xstr[:, slot], xr[:, ch * 8:(ch + 1) * 8, j * 128:(j + 1) * 128]))
            mk.wait("vector", tsem, trv)
            mk.wait("vector", lsem, sld)
            for k in range(8):
                ct = ch * 8 + k
                mk.op("vector", "v", 1, lambda e, k=k, ct=ct, slot=slot: e.tensor_scalar(
                    xstr[:, slot, k, :], xstr[:, slot, k, :], 1.0, a["gnbT"][:, ct:ct + 1],
                    ALU.mult, ALU.add))
            mk.wait("vector", "v", mk.cnt["v"])
            for k in range(8):
                ct = ch * 8 + k
                av = mk.op("vector", "v", 1, lambda e, k=k, ct=ct, slot=slot: e.scalar_tensor_tensor(
                    xstr[:, slot, k, :], xnT[:, slot, k, :], a["gnwT"][:, ct:ct + 1],
                    xstr[:, slot, k, :], ALU.mult, ALU.add))
            mk.wait("gpsimd", "v", av)
            stv = mk.op("gpsimd", ssem, 16, lambda e, ch=ch, j=j, slot=slot: e.dma_start(
                xdr[:, ch * 8:(ch + 1) * 8, j * 128:(j + 1) * 128], xstr[:, slot]))
            stripe_ops[seq] = (trv, stv, av)
            state["last_store"] = stv

    for ch in range(2):
        mk.wait("gpsimd", "t", state["qmm"])
        wq_done = mk.op("gpsimd", "wq", 16, lambda e, ch=ch: e.dma_start(
            wq[:], WQ[:, ch * 1024:(ch + 1) * 1024].rearrange("(k p) c -> p k c", p=128)))
        mk.wait("tensor", "wq", wq_done)
        for j in range(NT):
            emitQ(ch, j)
            emitA(ch, j)
            if j % 2 == 1:
                emitB1(ch, j - 1)
                emitB1(ch, j)
                emitB2(ch, j)
    last_store = state["last_store"]
    return last_store



from concourse.bass_utils import run_bass_kernel_spmd

NCORES = 8
ROPE_DIM = 64
_CACHE = {}


def _rope_tables():
    inv_freq = (1.0 / 10000.0 ** (np.arange(0, ROPE_DIM, 2, dtype=np.float32) / ROPE_DIM)).astype(np.float32)
    freqs = np.arange(TT, dtype=np.float32)[:, None] * inv_freq[None, :]
    freqs = np.repeat(freqs, 2, axis=-1)
    return np.cos(freqs).astype(np.float32), np.sin(freqs).astype(np.float32)


def build_program():
    nc = bass.Bass("TRN2", target_bir_lowering=False)
    x0 = nc.dram_tensor("x0", [TT, D], F32, kind="ExternalInput").ap()
    W13t = nc.dram_tensor("w13", [NU, 2, NI, 128, ND, 128], F32, kind="ExternalInput").ap()
    W2t = nc.dram_tensor("w2", [NU, 4, 128, NI, 512], F32, kind="ExternalInput").ap()
    GWTt = nc.dram_tensor("gwt", [D, 16], F32, kind="ExternalInput").ap()
    EBt = nc.dram_tensor("eb", [128, 16], F32, kind="ExternalInput").ap()
    WQt = nc.dram_tensor("wqd", [D, D], F32, kind="ExternalInput").ap()
    SMAt = nc.dram_tensor("sma", [128, D + 32 + 2 * NT * 64], F32, kind="ExternalInput").ap()
    Ct = nc.dram_tensor("consts", [128, 2], F32, kind="ExternalInput").ap()
    xm1 = nc.dram_tensor("xm1", [TT, D], F32, kind="Internal").ap()
    xm2 = nc.dram_tensor("xm2", [TT, D], F32, kind="Internal").ap()
    xo = nc.dram_tensor("xo", [TT, D], F32, kind="ExternalOutput").ap()

    mk = Mk(nc)
    with ExitStack() as es:
        for s in ("xa", "xb", "xc", "xd", "xsa", "xsb", "xsc", "xsd", "lta", "ltb", "ltc", "ltd",
                  "lt4", "lt5", "lt6", "lt7", "xe", "xf", "xg", "xh", "wr0", "wr1", "wr2", "wb0", "wb1", "wb2", "pre", "wq", "g", "t", "v", "s"):
            mk.new_sem(s, es)
        gwt_s = es.enter_context(nc.sbuf_tensor("gwt_s", [128, ND, 16], BF16))
        eb_s = es.enter_context(nc.sbuf_tensor("eb_s", [128, 16], F32))
        cp = es.enter_context(nc.sbuf_tensor("cp", [128, 2], F32))
        p1 = es.enter_context(nc.psum_tensor("p1", [128, 2048], F32))
        p2 = es.enter_context(nc.psum_tensor("p2", [128, 2048], F32))
        nc.const_aps.register(F32, EPS, cp[:, 0:1])
        nc.const_aps.register(F32, -0.5, cp[:, 1:2])

        with ExitStack() as es1:
            t = alloc_ffn_tensors(nc, es1, "f1")
            mk.op("gpsimd", "pre", 16, lambda e: e.dma_start(
                gwt_s[:], GWTt.rearrange("(k p) e -> p k e", p=128)))
            mk.op("gpsimd", "pre", 16, lambda e: e.dma_start(eb_s[:], EBt))
            pre = mk.op("gpsimd", "pre", 16, lambda e: e.dma_start(cp[:], Ct))
            mk.wait("tensor", "pre", pre)
            mk.wait("vector", "pre", pre)
            mk.wait("scalar", "pre", pre)
            emit_ffn_call(mk, t, x0, xm1, W13t, W2t, gwt_s, eb_s, p1, p2)
            mk.run_block()

        with ExitStack() as es2:
            a = alloc_attn_tensors(nc, es2)
            mk.wait_stores("sync")
            mk.wait_stores("gpsimd")
            emit_attn(mk, a, xm1, xm2, WQt, SMAt, p1, p2)
            mk.run_block()

        with ExitStack() as es3:
            t2 = alloc_ffn_tensors(nc, es3, "f2")
            mk.wait_stores("sync")
            mk.wait_stores("gpsimd")
            emit_ffn_call(mk, t2, xm2, xo, W13t, W2t, gwt_s, eb_s, p1, p2)
            mk.wait_stores("sync")
            mk.run_block()
    return nc


def _prep_weights(inp):
    w1, w3, w2 = inp["L0_ffn_w1"], inp["L0_ffn_w3"], inp["L0_ffn_w2"]
    sw1, sw3, sw2 = inp["L0_ffn_sw1"], inp["L0_ffn_sw3"], inp["L0_ffn_sw2"]
    w1u = np.concatenate([w1, sw1.reshape(D, 2, TI).transpose(1, 0, 2)], 0)
    w3u = np.concatenate([w3, sw3.reshape(D, 2, TI).transpose(1, 0, 2)], 0)
    w2u = np.concatenate([w2, sw2.reshape(2, TI, D)], 0)
    W13 = np.ascontiguousarray(np.stack([
        w1u.reshape(NU, ND, 128, NI, 128).transpose(0, 3, 2, 1, 4),
        w3u.reshape(NU, ND, 128, NI, 128).transpose(0, 3, 2, 1, 4)], axis=1))
    W2 = np.ascontiguousarray(w2u.reshape(NU, NI, 128, 4, 512).transpose(0, 3, 2, 1, 4))
    GWT = np.ascontiguousarray(inp["L0_ffn_gate_w"].T)
    EB = np.ascontiguousarray(np.tile(inp["L0_ffn_expert_bias"][None, :], (128, 1)))
    WQ = np.ascontiguousarray(inp["L0_attn_W_Q"])
    cos, sin = _rope_tables()
    cosl = cos.reshape(NT, 128, 64).transpose(1, 0, 2).reshape(128, NT * 64)
    sinl = sin.reshape(NT, 128, 64).transpose(1, 0, 2).reshape(128, NT * 64)
    SMA = np.ascontiguousarray(np.concatenate([
        np.tile(inp["L0_attn_hh_b"][None, :], (128, 1)),
        inp["L0_attn_gn_weight"].reshape(16, 128).T,
        inp["L0_attn_gn_bias"].reshape(16, 128).T,
        cosl, sinl], axis=1)).astype(np.float32)
    CONSTS = np.tile(np.array([[EPS, -0.5]], np.float32), (128, 1))
    return W13, W2, GWT, EB, WQ, SMA, CONSTS


def kernel(**inp):
    _ = inp.get("L0_attn_S_n")
    X = inp["L0_attn_X"]
    W13, W2, GWT, EB, WQ, SMA, CONSTS = _prep_weights(inp)
    if "nc" not in _CACHE:
        _CACHE["nc"] = build_program()
    nc = _CACHE["nc"]
    in_maps = [dict(x0=np.ascontiguousarray(X[c]), w13=W13, w2=W2, gwt=GWT, eb=EB,
                    wqd=WQ, sma=SMA, consts=CONSTS) for c in range(NCORES)]
    res = run_bass_kernel_spmd(nc, in_maps, core_ids=list(range(NCORES)))
    return np.stack([np.asarray(res.results[c]["xo"], dtype=np.float32) for c in range(NCORES)], axis=0)


# version 09
# speedup vs baseline: 1.1519x; 1.0177x over previous
import numpy as np
from contextlib import ExitStack
import concourse.bass as bass
import concourse.mybir as mybir

F32 = mybir.dt.float32
I32 = mybir.dt.int32
BF16 = mybir.dt.bfloat16
AF = mybir.ActivationFunctionType
ALU = mybir.AluOpType
AX = mybir.AxisListType

D = 2048
TI = 1408
NU = 18
NI = 11
ND = 16
TT = 2048
TH = 1024
NJ = 8
EPS = 1e-5
BIG = 1e30
NEG_BIG = -1e30
RING13 = 3

ENGS = ("tensor", "vector", "scalar", "sync", "gpsimd")


class Mk:
    def __init__(self, nc):
        self.nc = nc
        self.prog = {e: [] for e in ENGS}
        self.cnt = {}
        self.sem = {}
        self.bank_free = {}
        self.bank_ready = {}
        self.emitted_wait = {}
        self.w13_slot_free = [0] * RING13
        self.w2_slot_free = [0] * 3
        self.sil_free = [[0, 0], [0, 0]]
        self.w13_seq = 0
        self.w13_seq2 = 0
        from collections import deque
        self.w13_jobs = deque()
        self.w2_jobs = deque()
        self.w13_loaded = {}
        self.w2_loaded = {}
        self.w2_emit_seq = 0
        self.w2_cons_seq = 0
        self.call_seq = 0

    def new_sem(self, name, es):
        self.sem[name] = es.enter_context(self.nc.semaphore(name))
        self.cnt[name] = 0

    def emit(self, eng, fn):
        self.prog[eng].append(fn)

    def wait(self, eng, name, val):
        if val <= 0:
            return
        key = (eng, name)
        if val <= self.emitted_wait.get(key, -1):
            return
        self.emitted_wait[key] = val
        s = self.sem[name]
        self.emit(eng, lambda e, s=s, v=val: e.wait_ge(s, v))

    def wait_bank_free(self, eng, bank):
        if bank in self.bank_free:
            nm, v = self.bank_free[bank]
            self.wait(eng, nm, v)

    def wait_bank_ready(self, eng, bank):
        nm, v = self.bank_ready[bank]
        self.wait(eng, nm, v)

    def op(self, eng, name, n, fn):
        self.cnt[name] += n
        v = self.cnt[name]
        s = self.sem[name]
        self.emit(eng, lambda e, fn=fn, s=s, n=n: fn(e).then_inc(s, n))
        return v

    def raw(self, eng, fn):
        self.emit(eng, fn)

    def wait_stores(self, eng):
        for nm in ("xsa", "xsb", "xsc", "xsd"):
            self.wait(eng, nm, self.cnt[nm])

    def chain(self, eng, name, fn):
        self.wait(eng, name, self.cnt[name])
        return self.op(eng, name, 1, fn)

    def run_block(self):
        nc = self.nc
        with nc.Block() as block:
            @block.tensor
            def _(t):
                for f in self.prog["tensor"]:
                    f(t)

            @block.vector
            def _(v):
                for f in self.prog["vector"]:
                    f(v)

            @block.scalar
            def _(s):
                for f in self.prog["scalar"]:
                    f(s)

            @block.sync
            def _(sy):
                for f in self.prog["sync"]:
                    f(sy)

            @block.gpsimd
            def _(g):
                for f in self.prog["gpsimd"]:
                    f(g)
        self.prog = {e: [] for e in ENGS}


def alloc_ffn_tensors(nc, es, pfx=""):
    def sb(name, shape, dtype, **kw):
        return es.enter_context(nc.sbuf_tensor(pfx + name, shape, dtype, **kw))
    t = {}
    t["x_h"] = sb("x_h", [128, NJ, D], F32)
    t["lnxT"] = sb("lnxT", [128, ND, TH], BF16, is_transpose_dest=True)
    t["hT"] = sb("hT", [128, 2, NI, TH], BF16)
    t["w13r"] = sb("w13r", [128, RING13, 2, ND, 128], BF16)
    t["w2r"] = sb("w2r", [128, NI, 512], BF16)
    t["lnbf"] = sb("lnbf", [128, 2, D], BF16)
    t["sil"] = sb("sil", [128, 2, 2, 512], F32)
    t["scr"] = sb("scr", [128, D], F32)
    t["stat"] = sb("stat", [128, 8, NJ], F32)
    t["sgate"] = sb("sgate", [128, NJ, 16], F32)
    t["cw05"] = sb("cw05", [128, NJ, NU], F32)
    t["gt"] = sb("gt", [128, 8, 16], F32)
    t["gsml"] = sb("gsml", [128, 8], F32)
    return t


def emit_ln_pass(mk, t, x_src_t, half, lnxT, x_h):
    stat, scr, lnbf = t["stat"], t["scr"], t["lnbf"]
    xr = x_src_t.rearrange("(j p) d -> p j d", p=128)

    xload_done = []
    xload_sem = []
    mk.wait_stores("sync")
    for g in range(2):
        j0 = half * NJ + g * 4
        sem = "xa" if g == 0 else "xb"
        v = mk.op("sync", sem, 16, lambda e, g=g, j0=j0: e.dma_start(
            x_h[:, g * 4:(g + 1) * 4, :], xr[:, j0:j0 + 4, :]))
        xload_done.append(v)
        xload_sem.append(sem)

    red_done = []
    for j in range(NJ):
        mk.wait("vector", xload_sem[j // 4], xload_done[j // 4])
        mk.wait("scalar", xload_sem[j // 4], xload_done[j // 4])
        mk.op("vector", "v", 1, lambda e, j=j: e.reduce_sum(
            stat[:, 0, j:j + 1], x_h[:, j, :], axis=AX.X))
        sv = mk.chain("scalar", "s", lambda e, j=j: e.activation(
            scr[:], x_h[:, j, :], AF.Square, accum_out=stat[:, 1, j:j + 1]))
        red_done.append(sv)

    mk.wait("vector", "s", red_done[-1])
    mk.chain("vector", "v", lambda e: e.tensor_scalar(
        stat[:, 2, :], stat[:, 0, :], 1.0 / D, None, ALU.mult))
    mk.chain("vector", "v", lambda e: e.tensor_scalar(
        stat[:, 3, :], stat[:, 1, :], 1.0 / D, None, ALU.mult))
    mk.chain("vector", "v", lambda e: e.tensor_tensor(
        stat[:, 4, :], stat[:, 2, :], stat[:, 2, :], ALU.mult))
    vv = mk.chain("vector", "v", lambda e: e.tensor_sub(
        stat[:, 4, :], stat[:, 3, :], stat[:, 4, :]))
    mk.wait("scalar", "v", vv)
    mk.chain("scalar", "s", lambda e: e.activation(
        stat[:, 7, :], stat[:, 4, :], AF.Ln, bias=EPS))
    sv = mk.chain("scalar", "s", lambda e: e.activation(
        stat[:, 5, :], stat[:, 7, :], AF.Exp, scale=-0.5))
    mk.wait("vector", "s", sv)
    mk.chain("vector", "v", lambda e: e.tensor_tensor(
        stat[:, 6, :], stat[:, 2, :], stat[:, 5, :], ALU.mult))
    mk.chain("vector", "v", lambda e: e.tensor_scalar(
        stat[:, 6, :], stat[:, 6, :], -1.0, None, ALU.mult))

    tr_done = []
    for j in range(NJ):
        par = j % 2
        lsem = "lta" if par == 0 else "ltb"
        if j >= 2:
            mk.wait("vector", lsem, tr_done[j - 2])
        emitfn = mk.chain if j == 0 else (lambda eng, nm, f: mk.op(eng, nm, 1, f))
        vv = emitfn("vector", "v", lambda e, j=j, par=par: e.tensor_scalar(
            lnbf[:, par, :], x_h[:, j, :], stat[:, 5, j:j + 1], stat[:, 6, j:j + 1],
            ALU.mult, ALU.add))
        mk.wait("sync", "v", vv)
        tv = mk.op("sync", lsem, 16, lambda e, j=j, par=par: e.dma_start(
            lnxT[:, :, j * 128:(j + 1) * 128], lnbf[:, par, :], transpose=True))
        tr_done.append(tv)
    return tr_done


def emit_gate(mk, t, gwt_s, eb_s, p2, lnxT_tr_done):
    lnxT, sgate, cw05, gt, gsml = t["lnxT"], t["sgate"], t["cw05"], t["gt"], t["gsml"]
    mk.wait("tensor", "lta", mk.cnt["lta"])
    mk.wait("tensor", "ltb", mk.cnt["ltb"])
    sig_done = [0, 0]
    gate_mm_done = [0, 0]
    for j in range(NJ):
        q = j % 2
        pg = p2[:, q * 512:q * 512 + 16]
        if j >= 2:
            mk.wait("tensor", "s", sig_done[q])
        else:
            mk.wait_bank_free("tensor", ("p2", q))
        for d in range(ND):
            if d < ND - 1:
                mk.raw("tensor", lambda e, d=d, j=j, pg=pg: e.matmul(
                    pg, lnxT[:, d, j * 128:(j + 1) * 128], gwt_s[:, d, :],
                    start=(d == 0), stop=False))
            else:
                gate_mm_done[q] = mk.op("tensor", "t", 1, lambda e, d=d, j=j, pg=pg: e.matmul(
                    pg, lnxT[:, d, j * 128:(j + 1) * 128], gwt_s[:, d, :],
                    start=False, stop=True))
        mk.wait("scalar", "t", gate_mm_done[q])
        sig_done[q] = mk.op("scalar", "s", 1, lambda e, j=j, pg=pg: e.activation(
            sgate[:, j, :], pg, AF.Sigmoid))

        mk.wait("vector", "s", sig_done[q])
        mk.chain("vector", "v", lambda e, j=j: e.tensor_tensor(
            gt[:, 0, :], sgate[:, j, :], eb_s[:], ALU.add))
        mk.chain("vector", "v", lambda e: e.reduce_max(
            gsml[:, 0:4], gt[:, 0, :].rearrange("p (a b) -> p a b", a=4), axis=AX.X))
        mk.chain("vector", "v", lambda e: e.reduce_max(
            gsml[:, 4:5], gsml[:, 0:4], axis=AX.X))
        mk.chain("vector", "v", lambda e: e.tensor_scalar(
            gt[:, 7, 0:4], gsml[:, 0:4], gsml[:, 4:5], None, ALU.is_ge))
        mk.chain("vector", "v", lambda e: e.scalar_tensor_tensor(
            gt[:, 7, 4:8], gt[:, 7, 0:4], NEG_BIG, gsml[:, 0:4], ALU.mult, ALU.add))
        mk.chain("vector", "v", lambda e: e.reduce_max(
            gsml[:, 4:5], gt[:, 7, 4:8], axis=AX.X))
        mk.chain("vector", "v", lambda e: e.tensor_scalar(
            gt[:, 7, 0:4], gsml[:, 0:4], gsml[:, 4:5], None, ALU.is_ge))
        mk.chain("vector", "v", lambda e: e.tensor_copy(
            gt[:, 1, :].rearrange("p (a b) -> p a b", a=4),
            gt[:, 7, 0:4].unsqueeze(2).broadcast_to((128, 4, 4))))
        mk.chain("vector", "v", lambda e: e.tensor_tensor(
            gt[:, 2, :], gt[:, 0, :], gt[:, 1, :], ALU.mult))
        mk.chain("vector", "v", lambda e: e.tensor_scalar(
            gt[:, 3, :], gt[:, 1, :], 1.0, None, ALU.subtract))
        mk.chain("vector", "v", lambda e: e.scalar_tensor_tensor(
            gt[:, 3, :], gt[:, 3, :], BIG, gt[:, 2, :], ALU.mult, ALU.add))
        mk.chain("vector", "v", lambda e: e.tensor_copy(gt[:, 4, :], gt[:, 3, :]))
        for k in range(3):
            a = 4 + (k % 2)
            b = 4 + ((k + 1) % 2)
            mk.chain("vector", "v", lambda e, a=a: e.reduce_max(
                gsml[:, 4:5], gt[:, a, :], axis=AX.X))
            mk.chain("vector", "v", lambda e, a=a: e.tensor_scalar(
                gt[:, 6, :], gt[:, a, :], gsml[:, 4:5], None, ALU.is_ge))
            mk.chain("vector", "v", lambda e, a=a, b=b: e.scalar_tensor_tensor(
                gt[:, b, :], gt[:, 6, :], NEG_BIG, gt[:, a, :], ALU.mult, ALU.add))
        mk.chain("vector", "v", lambda e: e.reduce_max(
            gsml[:, 4:5], gt[:, 5, :], axis=AX.X))
        mk.chain("vector", "v", lambda e: e.tensor_scalar(
            gt[:, 6, :], gt[:, 3, :], gsml[:, 4:5], None, ALU.is_ge))
        mk.chain("vector", "v", lambda e, j=j: e.tensor_tensor(
            gt[:, 6, :], sgate[:, j, :], gt[:, 6, :], ALU.mult))
        mk.chain("vector", "v", lambda e: e.reduce_sum(
            gsml[:, 5:6], gt[:, 6, :], axis=AX.X))
        mk.chain("vector", "v", lambda e: e.reciprocal(gsml[:, 6:7], gsml[:, 5:6]))
        mk.chain("vector", "v", lambda e: e.tensor_scalar(
            gsml[:, 6:7], gsml[:, 6:7], 0.5, None, ALU.mult))
        mk.chain("vector", "v", lambda e, j=j: e.tensor_scalar(
            cw05[:, j, 0:16], gt[:, 6, :], gsml[:, 6:7], None, ALU.mult))
    gv = mk.op("vector", "v", 1, lambda e: e.memset(cw05[:, :, 16:18], 0.5))
    mk.bank_free[("p2", 0)] = ("s", sig_done[0])
    mk.bank_free[("p2", 1)] = ("s", sig_done[1])
    return gv


def emit_ffn_units(mk, t, W13, W2, p1, p2, half, cw_ready, x_dst_t, first_half_of_program):
    x_h, lnxT, hT, w13r, w2r, sil, cw05 = (
        t["x_h"], t["lnxT"], t["hT"], t["w13r"], t["w2r"], t["sil"], t["cw05"])

    W13r = [[W13[u, m] for m in range(2)] for u in range(NU)]

    mk.call_seq += 1
    cs = mk.call_seq
    for u in range(NU):
        for i in range(NI):
            mk.w13_jobs.append((cs, u, i, W13r))
        for dq in range(4):
            for bi, (s0, bn) in enumerate(((0, 4), (4, 4), (8, 3))):
                mk.w2_jobs.append((cs, u, dq, bi, s0, bn, W2))

    def pump_w13():
        if not mk.w13_jobs:
            return
        jcs, u, i, Wr = mk.w13_jobs.popleft()
        slot = mk.w13_seq % RING13
        mk.w13_seq += 1
        mk.wait("gpsimd", "t", mk.w13_slot_free[slot])
        for m in range(2):
            v = mk.op("gpsimd", "wr%d" % slot, 16, lambda e, u=u, m=m, i=i, slot=slot, Wr=Wr: e.dma_start(
                w13r[:, slot, m], Wr[u][m][i]))
        mk.w13_loaded[(jcs, u, i)] = (slot, v)

    def pump_w2():
        if not mk.w2_jobs:
            return
        jcs, u, dq, bi, s0, bn, W2a = mk.w2_jobs.popleft()
        mk.wait("gpsimd", "t", mk.w2_slot_free[bi])
        v = mk.op("gpsimd", "wb%d" % bi, 16, lambda e, u=u, dq=dq, s0=s0, bn=bn, W2a=W2a: e.dma_start(
            w2r[:, s0:s0 + bn, :], W2a[u, dq, :, s0:s0 + bn, :]))
        mk.w2_loaded[(jcs, u, dq, bi)] = v
        mk.w2_emit_seq += 1

    while mk.w13_seq < mk.w13_seq2 + RING13 and mk.w13_jobs:
        pump_w13()
    while mk.w2_emit_seq < mk.w2_cons_seq + 3 and mk.w2_jobs:
        pump_w2()

    hT_ready = {}
    sil_free = mk.sil_free
    unit_final_stt = {}
    cw_waited = [False]

    def emit_p1_block(u, i):
        ubuf = u % 2
        slot = mk.w13_seq2 % RING13
        mk.w13_seq2 += 1
        _wslot, _wval = mk.w13_loaded[(cs, u, i)]
        mk.wait("tensor", "wr%d" % _wslot, _wval)
        par = i % 2
        last_mm = 0
        for m in range(2):
            for t2 in range(2):
                bank = ("p1", m * 2 + t2)
                mk.wait_bank_free("tensor", bank)
                for d in range(ND):
                    if d < ND - 1:
                        mk.raw("tensor", lambda e, slot=slot, m=m, t2=t2, d=d: e.matmul(
                            p1[:, (m * 2 + t2) * 512:(m * 2 + t2 + 1) * 512],
                            w13r[:, slot, m, d, :], lnxT[:, d, t2 * 512:(t2 + 1) * 512],
                            start=(d == 0), stop=False))
                    else:
                        last_mm = mk.op("tensor", "t", 1, lambda e, slot=slot, m=m, t2=t2, d=d: e.matmul(
                            p1[:, (m * 2 + t2) * 512:(m * 2 + t2 + 1) * 512],
                            w13r[:, slot, m, d, :], lnxT[:, d, t2 * 512:(t2 + 1) * 512],
                            start=False, stop=True))
                mk.bank_ready[bank] = ("t", last_mm)
        mk.w13_slot_free[slot] = last_mm
        pump_w13()
        for t2 in range(2):
            mk.wait_bank_ready("scalar", ("p1", t2))
            mk.wait("scalar", "v", sil_free[par][t2])
            sv = mk.op("scalar", "s", 1, lambda e, par=par, t2=t2: e.activation(
                sil[:, par, t2, :], p1[:, t2 * 512:(t2 + 1) * 512], AF.Silu))
            mk.bank_free[("p1", t2)] = ("s", sv)
            mk.wait_bank_ready("vector", ("p1", 2 + t2))
            mk.wait("vector", "s", sv)
            vv = mk.op("vector", "v", 1, lambda e, ubuf=ubuf, i=i, par=par, t2=t2: e.tensor_tensor(
                hT[:, ubuf, i, t2 * 512:(t2 + 1) * 512], sil[:, par, t2, :],
                p1[:, (2 + t2) * 512:(3 + t2) * 512], ALU.mult))
            mk.bank_free[("p1", 2 + t2)] = ("v", vv)
            sil_free[par][t2] = vv
        hT_ready[(u, i)] = vv

    def emit_p2_block(u, dq, jg):
        ubuf = u % 2
        if not cw_waited[0]:
            mk.wait("vector", "v", cw_ready)
            cw_waited[0] = True
        last_mm = 0
        for jj in range(4):
            mk.wait_bank_free("tensor", ("p2", jj))
        for i in range(NI):
            bi = 0 if i < 4 else (1 if i < 8 else 2)
            mk.wait("tensor", "wb%d" % bi, mk.w2_loaded[(cs, u, dq, bi)])
            if dq == 0 and jg == 0:
                mk.wait("tensor", "v", hT_ready[(u, i)])
            boundary = (jg == 1 and i in (3, 7, 10))
            for jj in range(4):
                j = jg * 4 + jj
                def _mm(e, ubuf=ubuf, i=i, j=j, jj=jj):
                    return e.matmul(
                        p2[:, jj * 512:(jj + 1) * 512],
                        hT[:, ubuf, i, j * 128:(j + 1) * 128], w2r[:, i, :],
                        start=(i == 0), stop=(i == NI - 1))
                if jj == 3 and (boundary or i == NI - 1):
                    lm = mk.op("tensor", "t", 1, _mm)
                    if i == NI - 1:
                        last_mm = lm
                else:
                    mk.raw("tensor", _mm)
            if boundary:
                mk.w2_slot_free[{3: 0, 7: 1, 10: 2}[i]] = lm
                mk.w2_cons_seq += 1
                pump_w2()
        for jj in range(4):
            j = jg * 4 + jj
            mk.wait("vector", "t", last_mm)
            vv = mk.op("vector", "v", 1, lambda e, jj=jj, j=j, dq=dq, u=u: e.scalar_tensor_tensor(
                x_h[:, j, dq * 512:(dq + 1) * 512], p2[:, jj * 512:(jj + 1) * 512],
                cw05[:, j, u:u + 1], x_h[:, j, dq * 512:(dq + 1) * 512],
                ALU.mult, ALU.add))
            mk.bank_free[("p2", jj)] = ("v", vv)
            if u == NU - 1 and dq == 3:
                unit_final_stt[jg] = vv

    for i in range(NI):
        emit_p1_block(0, i)
    for u in range(NU):
        p2_blocks = [(dq, jg) for dq in range(4) for jg in range(2)]
        p1_blocks = list(range(NI)) if u + 1 < NU else []
        n2 = 0
        for k, i in enumerate(p1_blocks):
            while n2 < len(p2_blocks) and n2 < (k * 8) // 10 + 1:
                emit_p2_block(u, *p2_blocks[n2])
                n2 += 1
            emit_p1_block(u + 1, i)
        while n2 < len(p2_blocks):
            emit_p2_block(u, *p2_blocks[n2])
            n2 += 1

    xdr = x_dst_t.rearrange("(j p) d -> p j d", p=128)
    store_done = []
    for jg in range(2):
        sem = "xsa" if jg == 0 else "xsb"
        mk.wait("sync", "v", unit_final_stt[jg])
        v = mk.op("sync", sem, 16, lambda e, jg=jg, half=half: e.dma_start(
            xdr[:, half * NJ + jg * 4: half * NJ + jg * 4 + 4, :],
            x_h[:, jg * 4:(jg + 1) * 4, :]))
        store_done.append(v)
    return store_done


def emit_ffn_call(mk, t, x_src_t, x_dst_t, W13, W2, gwt_s, eb_s, p1, p2):
    stores = []
    for half in range(2):
        tr_done = emit_ln_pass(mk, t, x_src_t, half, t["lnxT"], t["x_h"])
        cw_ready = emit_gate(mk, t, gwt_s, eb_s, p2, tr_done)
        st = emit_ffn_units(mk, t, W13, W2, p1, p2, half, cw_ready, x_dst_t, half == 0)
        stores += st
    return stores



NT = 16


def alloc_attn_tensors(nc, es):
    a = {}
    a["ln2T"] = es.enter_context(nc.sbuf_tensor("ln2T", [128, ND, TT], BF16, is_transpose_dest=True))
    a["wq"] = es.enter_context(nc.sbuf_tensor("wq", [128, ND, 1024], BF16))
    a["u4"] = es.enter_context(nc.sbuf_tensor("u4", [128, 8192], F32))
    a["lnbf2"] = es.enter_context(nc.sbuf_tensor("lnbf2", [128, 2, D], BF16))
    a["stat2"] = es.enter_context(nc.sbuf_tensor("stat2", [128, 8, NT], F32))
    a["scr2"] = es.enter_context(nc.sbuf_tensor("scr2", [128, 4, 1024], F32))
    a["xnb"] = es.enter_context(nc.sbuf_tensor("xnb", [128, 4, 1024], BF16))
    a["xnT"] = es.enter_context(nc.sbuf_tensor("xnT", [128, 4, 8, 128], BF16, is_transpose_dest=True))
    a["xstr"] = es.enter_context(nc.sbuf_tensor("xstr", [128, 4, 8, 128], F32))
    a["gst"] = es.enter_context(nc.sbuf_tensor("gst", [128, 8, 64], F32))
    a["rtmp"] = es.enter_context(nc.sbuf_tensor("rtmp", [128, 3, 64], F32))
    a["hhb"] = es.enter_context(nc.sbuf_tensor("hhb", [128, D], F32))
    a["gnwT"] = es.enter_context(nc.sbuf_tensor("gnwT", [128, 16], F32))
    a["gnbT"] = es.enter_context(nc.sbuf_tensor("gnbT", [128, 16], F32))
    a["cosb"] = es.enter_context(nc.sbuf_tensor("cosb", [128, NT, 64], F32))
    a["sinb"] = es.enter_context(nc.sbuf_tensor("sinb", [128, NT, 64], F32))
    return a


def emit_attn(mk, a, x_src_t, x_dst_t, WQ, SMA, p1, p2):
    ln2T, wq, lnbf2, stat2, scr2 = a["ln2T"], a["wq"], a["lnbf2"], a["stat2"], a["scr2"]
    xnb, xnT, xstr, gst, rtmp = a["xnb"], a["xnT"], a["xstr"], a["gst"], a["rtmp"]
    xtmp = a["u4"][:, 0:8192].rearrange("p (r d) -> p r d", r=4)
    Hb = a["u4"][:, 0:8192].rearrange("p (r d) -> p r d", r=8)

    sm_done = mk.op("gpsimd", "pre", 16, lambda e: e.dma_start(a["hhb"][:], SMA[:, 0:D]))
    sm_done = mk.op("gpsimd", "pre", 16, lambda e: e.dma_start(a["gnwT"][:], SMA[:, D:D + 16]))
    sm_done = mk.op("gpsimd", "pre", 16, lambda e: e.dma_start(a["gnbT"][:], SMA[:, D + 16:D + 32]))
    sm_done = mk.op("gpsimd", "pre", 16, lambda e: e.dma_start(
        a["cosb"][:], SMA[:, D + 32:D + 32 + NT * 64].rearrange("p (j c) -> p j c", c=64)))
    sm_done = mk.op("gpsimd", "pre", 16, lambda e: e.dma_start(
        a["sinb"][:], SMA[:, D + 32 + NT * 64:D + 32 + 2 * NT * 64].rearrange("p (j c) -> p j c", c=64)))

    xr = x_src_t.rearrange("(j p) d -> p j d", p=128)
    xdr = x_dst_t.rearrange("(j p) d -> p j d", p=128)

    MAGIC = 0x5F3759DF
    norm_done = [0] * NT
    redv_done = [0] * NT
    reds_done = [0] * NT
    tr_done = [0] * NT
    for j in range(NT):
        slot = j % 4
        xsem = ("xa", "xb", "xc", "xd")[slot]
        lsem = "lta" if j % 2 == 0 else "ltb"
        if j >= 4:
            mk.wait("sync", "v", norm_done[j - 4])
        ld = mk.op("sync", xsem, 16, lambda e, j=j, slot=slot: e.dma_start(
            xtmp[:, slot, :], xr[:, j, :]))
        mk.wait("vector", xsem, ld)
        mk.wait("scalar", xsem, ld)
        redv_done[j] = mk.op("vector", "v", 1, lambda e, j=j, slot=slot: e.reduce_sum(
            stat2[:, 0, j:j + 1], xtmp[:, slot, :], axis=AX.X))
        sv = mk.chain("scalar", "s", lambda e, j=j, slot=slot: e.activation(
            scr2[:, 0, :], xtmp[:, slot, 0:1024], AF.Square, accum_out=stat2[:, 2, j:j + 1]))
        sv = mk.chain("scalar", "s", lambda e, j=j, slot=slot: e.activation(
            scr2[:, 0, :], xtmp[:, slot, 1024:2048], AF.Square, accum_out=stat2[:, 3, j:j + 1]))
        reds_done[j] = sv
        if j % 2 == 0:
            continue
        jj = j - 1
        mk.wait("vector", "s", sv)
        mk.chain("vector", "v", lambda e, jj=jj: e.tensor_add(
            stat2[:, 1, jj:jj + 2], stat2[:, 2, jj:jj + 2], stat2[:, 3, jj:jj + 2]))
        mk.chain("vector", "v", lambda e, jj=jj: e.tensor_scalar(
            stat2[:, 2:4:1, jj:jj + 2] if False else stat2[:, 2, jj:jj + 2],
            stat2[:, 0, jj:jj + 2], 1.0 / D, None, ALU.mult))
        mk.chain("vector", "v", lambda e, jj=jj: e.tensor_scalar(
            stat2[:, 3, jj:jj + 2], stat2[:, 1, jj:jj + 2], 1.0 / D, None, ALU.mult))
        mk.chain("vector", "v", lambda e, jj=jj: e.tensor_tensor(
            stat2[:, 4, jj:jj + 2], stat2[:, 2, jj:jj + 2], stat2[:, 2, jj:jj + 2], ALU.mult))
        mk.chain("vector", "v", lambda e, jj=jj: e.tensor_sub(
            stat2[:, 4, jj:jj + 2], stat2[:, 3, jj:jj + 2], stat2[:, 4, jj:jj + 2]))
        mk.chain("vector", "v", lambda e, jj=jj: e.tensor_scalar(
            stat2[:, 7, jj:jj + 2], stat2[:, 4, jj:jj + 2], EPS, None, ALU.add))
        mk.chain("vector", "v", lambda e, jj=jj: e.tensor_scalar(
            stat2[:, 5, jj:jj + 2].bitcast(I32), stat2[:, 7, jj:jj + 2].bitcast(I32), 1, None,
            ALU.logical_shift_right))
        mk.chain("vector", "v", lambda e, jj=jj: e.tensor_scalar(
            stat2[:, 5, jj:jj + 2].bitcast(I32), stat2[:, 5, jj:jj + 2].bitcast(I32), -1, MAGIC,
            ALU.mult, ALU.add))
        for _ in range(3):
            mk.chain("vector", "v", lambda e, jj=jj: e.tensor_tensor(
                stat2[:, 6, jj:jj + 2], stat2[:, 5, jj:jj + 2], stat2[:, 5, jj:jj + 2], ALU.mult))
            mk.chain("vector", "v", lambda e, jj=jj: e.tensor_tensor(
                stat2[:, 6, jj:jj + 2], stat2[:, 6, jj:jj + 2], stat2[:, 7, jj:jj + 2], ALU.mult))
            mk.chain("vector", "v", lambda e, jj=jj: e.tensor_scalar(
                stat2[:, 6, jj:jj + 2], stat2[:, 6, jj:jj + 2], -0.5, 1.5, ALU.mult, ALU.add))
            mk.chain("vector", "v", lambda e, jj=jj: e.tensor_tensor(
                stat2[:, 5, jj:jj + 2], stat2[:, 5, jj:jj + 2], stat2[:, 6, jj:jj + 2], ALU.mult))
        mk.chain("vector", "v", lambda e, jj=jj: e.scalar_tensor_tensor(
            stat2[:, 6, jj:jj + 2], stat2[:, 2, jj:jj + 2], -1.0, stat2[:, 5, jj:jj + 2],
            ALU.mult, ALU.mult))
        for jn in (jj, j):
            nslot = jn % 4
            nlsem = "lta" if jn % 2 == 0 else "ltb"
            npar = jn % 2
            if jn >= 2:
                mk.wait("vector", nlsem, tr_done[jn - 2])
            norm_done[jn] = mk.chain("vector", "v", lambda e, jn=jn, nslot=nslot, npar=npar: e.tensor_scalar(
                lnbf2[:, npar, :], xtmp[:, nslot, :], stat2[:, 5, jn:jn + 1], stat2[:, 6, jn:jn + 1],
                ALU.mult, ALU.add))
            mk.wait("sync", "v", norm_done[jn])
            tr_done[jn] = mk.op("sync", nlsem, 16, lambda e, jn=jn, npar=npar: e.dma_start(
                ln2T[:, :, jn * 128:(jn + 1) * 128], lnbf2[:, npar, :], transpose=True))

    mk.wait("vector", "pre", sm_done)
    mk.wait("gpsimd", "pre", sm_done)
    stripe_ops = {}
    tanh_done = [0, 0, 0, 0]
    gsq_done = [0, 0, 0, 0]
    slices_last = [0, 0]
    qmm_of = {}
    relu_of = {}
    state = {"last_store": 0, "qmm": 0}

    def emitQ(ch, j):
        par = j % 2
        pset = p1 if (j % 4) < 2 else p2
        pq = pset[:, par * 1024:(par + 1) * 1024]
        mk.wait("tensor", "lta" if par == 0 else "ltb", tr_done[j])
        mk.wait_bank_free("tensor", ("pq", j % 4))
        for d in range(ND):
            for c2 in range(2):
                fn = lambda e, d=d, c2=c2, j=j, pq=pq: e.matmul(
                    pq[:, c2 * 512:(c2 + 1) * 512], ln2T[:, d, j * 128:(j + 1) * 128],
                    wq[:, d, c2 * 512:(c2 + 1) * 512],
                    start=(d == 0), stop=(d == ND - 1))
                if d == ND - 1 and c2 == 1:
                    state["qmm"] = mk.op("tensor", "t", 1, fn)
                else:
                    mk.raw("tensor", fn)
        qmm_of[(ch, j)] = state["qmm"]
        mk.bank_ready[("pq", j % 4)] = ("t", state["qmm"])

    def emitA(ch, j):
        par = j % 2
        pset = p1 if (j % 4) < 2 else p2
        pq = pset[:, par * 1024:(par + 1) * 1024]
        H = Hb[:, j % 4, :]
        mk.wait("vector", "t", qmm_of[(ch, j)])
        if ch == 0:
            mk.chain("vector", "v", lambda e, pq=pq: e.tensor_scalar(
                rtmp[:, 0, 0:64:2], pq[:, 1:64:2], -1.0, None, ALU.mult))
            mk.chain("vector", "v", lambda e, pq=pq: e.tensor_copy(
                rtmp[:, 0, 1:64:2], pq[:, 0:64:2]))
            mk.chain("vector", "v", lambda e, j=j: e.tensor_tensor(
                rtmp[:, 1, :], rtmp[:, 0, :], a["sinb"][:, j, :], ALU.mult))
            mk.chain("vector", "v", lambda e, pq=pq, j=j: e.tensor_tensor(
                rtmp[:, 2, :], pq[:, 0:64], a["cosb"][:, j, :], ALU.mult))
            mk.chain("vector", "v", lambda e, pq=pq: e.tensor_add(
                pq[:, 0:64], rtmp[:, 1, :], rtmp[:, 2, :]))
        mk.wait("vector", "s", tanh_done[j % 4])
        rd = mk.chain("vector", "v", lambda e, pq=pq, ch=ch, H=H: e.scalar_tensor_tensor(
            H, pq[:], 0.0, a["hhb"][:, ch * 1024:(ch + 1) * 1024], ALU.max, ALU.add))
        mk.bank_free[("pq", j % 4)] = ("v", rd)
        relu_of[(ch, j)] = rd

    def emitB1(ch, j):
        par = j % 2
        q4 = j % 4
        H = Hb[:, q4, :]
        Ts = Hb[:, 4 + q4, :]
        go = (j % 4) * 16
        mk.wait("scalar", "v", relu_of[(ch, j)])
        mk.wait("scalar", "v", gsq_done[q4])
        tv = mk.chain("scalar", "s", lambda e, H=H, Ts=Ts: e.activation(Ts, H, AF.Tanh))
        tanh_done[q4] = tv
        sq = mk.chain("scalar", "s", lambda e, Ts=Ts, q4=q4: e.activation(
            scr2[:, q4, :], Ts, AF.Square))
        mk.wait("vector", "s", sq)
        mk.chain("vector", "v", lambda e, Ts=Ts, go=go: e.reduce_sum(
            gst[:, 0, go:go + 16], Ts.rearrange("p (g c) -> p g c", c=64), axis=AX.X))
        gq = mk.chain("vector", "v", lambda e, q4=q4, go=go: e.reduce_sum(
            gst[:, 1, go:go + 16], scr2[:, q4, :].rearrange("p (g c) -> p g c", c=64), axis=AX.X))
        gsq_done[q4] = gq

    MAGIC = 0x5F3759DF

    def emitB2(ch, jodd):
        ps = ((jodd // 2) % 2)
        s0 = ps * 32
        sl32 = slice(s0, s0 + 32)
        mk.wait("vector", "s", slices_last[ps])
        mk.chain("vector", "v", lambda e: e.tensor_scalar(
            gst[:, 2:4, sl32], gst[:, 0:2, sl32], 1.0 / 64, None, ALU.mult))
        mk.chain("vector", "v", lambda e: e.tensor_tensor(
            gst[:, 4, sl32], gst[:, 2, sl32], gst[:, 2, sl32], ALU.mult))
        mk.chain("vector", "v", lambda e: e.tensor_sub(
            gst[:, 4, sl32], gst[:, 3, sl32], gst[:, 4, sl32]))
        mk.chain("vector", "v", lambda e: e.tensor_scalar(
            gst[:, 7, sl32], gst[:, 4, sl32], EPS, None, ALU.add))
        mk.chain("vector", "v", lambda e: e.tensor_scalar(
            gst[:, 5, sl32].bitcast(I32), gst[:, 7, sl32].bitcast(I32), 1, None,
            ALU.logical_shift_right))
        mk.chain("vector", "v", lambda e: e.tensor_scalar(
            gst[:, 5, sl32].bitcast(I32), gst[:, 5, sl32].bitcast(I32), -1, MAGIC,
            ALU.mult, ALU.add))
        for _ in range(3):
            mk.chain("vector", "v", lambda e: e.tensor_tensor(
                gst[:, 3, sl32], gst[:, 5, sl32], gst[:, 5, sl32], ALU.mult))
            mk.chain("vector", "v", lambda e: e.tensor_tensor(
                gst[:, 3, sl32], gst[:, 3, sl32], gst[:, 7, sl32], ALU.mult))
            mk.chain("vector", "v", lambda e: e.tensor_scalar(
                gst[:, 3, sl32], gst[:, 3, sl32], -0.5, 1.5, ALU.mult, ALU.add))
            mk.chain("vector", "v", lambda e: e.tensor_tensor(
                gst[:, 5, sl32], gst[:, 5, sl32], gst[:, 3, sl32], ALU.mult))
        nm = mk.chain("vector", "v", lambda e: e.scalar_tensor_tensor(
            gst[:, 6, sl32], gst[:, 2, sl32], -1.0, gst[:, 5, sl32], ALU.mult, ALU.mult))
        mk.wait("scalar", "v", nm)
        for j in (jodd - 1, jodd):
            seq = ch * NT + j
            par = j % 2
            slot = seq % 4
            tsem = ("lt4", "lt5", "lt6", "lt7")[slot]
            lsem = ("xe", "xf", "xg", "xh")[slot]
            ssem = ("xsa", "xsb", "xsc", "xsd")[slot]
            Ts = Hb[:, 4 + (j % 4), :]
            go = (j % 4) * 16
            if seq - 4 in stripe_ops:
                mk.wait("scalar", tsem, stripe_ops[seq - 4][0])
            for g in range(16):
                sl = mk.op("scalar", "s", 1, lambda e, g=g, go=go, slot=slot, Ts=Ts: e.activation(
                    xnb[:, slot, g * 64:(g + 1) * 64], Ts[:, g * 64:(g + 1) * 64], AF.Identity,
                    bias=gst[:, 6, go + g:go + g + 1], scale=gst[:, 5, go + g:go + g + 1]))
            slices_last[ps] = sl
            mk.wait("sync", "s", sl)
            if seq - 4 in stripe_ops:
                mk.wait("sync", "v", stripe_ops[seq - 4][2])
            trv = mk.op("sync", tsem, 16, lambda e, slot=slot: e.dma_start(
                xnT[:, slot], xnb[:, slot, :], transpose=True))
            if seq - 4 in stripe_ops:
                mk.wait("gpsimd", ssem, stripe_ops[seq - 4][1])
            sld = mk.op("gpsimd", lsem, 16, lambda e, ch=ch, j=j, slot=slot: e.dma_start(
                xstr[:, slot], xr[:, ch * 8:(ch + 1) * 8, j * 128:(j + 1) * 128]))
            mk.wait("vector", tsem, trv)
            mk.wait("vector", lsem, sld)
            for k in range(8):
                ct = ch * 8 + k
                mk.op("vector", "v", 1, lambda e, k=k, ct=ct, slot=slot: e.tensor_scalar(
                    xstr[:, slot, k, :], xstr[:, slot, k, :], 1.0, a["gnbT"][:, ct:ct + 1],
                    ALU.mult, ALU.add))
            mk.wait("vector", "v", mk.cnt["v"])
            for k in range(8):
                ct = ch * 8 + k
                av = mk.op("vector", "v", 1, lambda e, k=k, ct=ct, slot=slot: e.scalar_tensor_tensor(
                    xstr[:, slot, k, :], xnT[:, slot, k, :], a["gnwT"][:, ct:ct + 1],
                    xstr[:, slot, k, :], ALU.mult, ALU.add))
            mk.wait("gpsimd", "v", av)
            stv = mk.op("gpsimd", ssem, 16, lambda e, ch=ch, j=j, slot=slot: e.dma_start(
                xdr[:, ch * 8:(ch + 1) * 8, j * 128:(j + 1) * 128], xstr[:, slot]))
            stripe_ops[seq] = (trv, stv, av)
            state["last_store"] = stv

    for ch in range(2):
        mk.wait("gpsimd", "t", state["qmm"])
        wq_done = mk.op("gpsimd", "wq", 16, lambda e, ch=ch: e.dma_start(
            wq[:], WQ[:, ch * 1024:(ch + 1) * 1024].rearrange("(k p) c -> p k c", p=128)))
        mk.wait("tensor", "wq", wq_done)
        for j in range(NT):
            emitQ(ch, j)
            emitA(ch, j)
            if j % 2 == 1:
                emitB1(ch, j - 1)
                emitB1(ch, j)
                if j >= 3:
                    emitB2(ch, j - 2)
        emitB2(ch, NT - 1)
    last_store = state["last_store"]
    return last_store



from concourse.bass_utils import run_bass_kernel_spmd

NCORES = 8
ROPE_DIM = 64
_CACHE = {}


def _rope_tables():
    inv_freq = (1.0 / 10000.0 ** (np.arange(0, ROPE_DIM, 2, dtype=np.float32) / ROPE_DIM)).astype(np.float32)
    freqs = np.arange(TT, dtype=np.float32)[:, None] * inv_freq[None, :]
    freqs = np.repeat(freqs, 2, axis=-1)
    return np.cos(freqs).astype(np.float32), np.sin(freqs).astype(np.float32)


def build_program():
    nc = bass.Bass("TRN2", target_bir_lowering=False)
    x0 = nc.dram_tensor("x0", [TT, D], F32, kind="ExternalInput").ap()
    W13t = nc.dram_tensor("w13", [NU, 2, NI, 128, ND, 128], F32, kind="ExternalInput").ap()
    W2t = nc.dram_tensor("w2", [NU, 4, 128, NI, 512], F32, kind="ExternalInput").ap()
    GWTt = nc.dram_tensor("gwt", [D, 16], F32, kind="ExternalInput").ap()
    EBt = nc.dram_tensor("eb", [128, 16], F32, kind="ExternalInput").ap()
    WQt = nc.dram_tensor("wqd", [D, D], F32, kind="ExternalInput").ap()
    SMAt = nc.dram_tensor("sma", [128, D + 32 + 2 * NT * 64], F32, kind="ExternalInput").ap()
    Ct = nc.dram_tensor("consts", [128, 2], F32, kind="ExternalInput").ap()
    xm1 = nc.dram_tensor("xm1", [TT, D], F32, kind="Internal").ap()
    xm2 = nc.dram_tensor("xm2", [TT, D], F32, kind="Internal").ap()
    xo = nc.dram_tensor("xo", [TT, D], F32, kind="ExternalOutput").ap()

    mk = Mk(nc)
    with ExitStack() as es:
        for s in ("xa", "xb", "xc", "xd", "xsa", "xsb", "xsc", "xsd", "lta", "ltb", "ltc", "ltd",
                  "lt4", "lt5", "lt6", "lt7", "xe", "xf", "xg", "xh", "wr0", "wr1", "wr2", "wb0", "wb1", "wb2", "pre", "wq", "g", "t", "v", "s"):
            mk.new_sem(s, es)
        gwt_s = es.enter_context(nc.sbuf_tensor("gwt_s", [128, ND, 16], BF16))
        eb_s = es.enter_context(nc.sbuf_tensor("eb_s", [128, 16], F32))
        cp = es.enter_context(nc.sbuf_tensor("cp", [128, 2], F32))
        p1 = es.enter_context(nc.psum_tensor("p1", [128, 2048], F32))
        p2 = es.enter_context(nc.psum_tensor("p2", [128, 2048], F32))
        nc.const_aps.register(F32, EPS, cp[:, 0:1])
        nc.const_aps.register(F32, -0.5, cp[:, 1:2])

        with ExitStack() as es1:
            t = alloc_ffn_tensors(nc, es1, "f1")
            mk.op("gpsimd", "pre", 16, lambda e: e.dma_start(
                gwt_s[:], GWTt.rearrange("(k p) e -> p k e", p=128)))
            mk.op("gpsimd", "pre", 16, lambda e: e.dma_start(eb_s[:], EBt))
            pre = mk.op("gpsimd", "pre", 16, lambda e: e.dma_start(cp[:], Ct))
            mk.wait("tensor", "pre", pre)
            mk.wait("vector", "pre", pre)
            mk.wait("scalar", "pre", pre)
            emit_ffn_call(mk, t, x0, xm1, W13t, W2t, gwt_s, eb_s, p1, p2)
            mk.run_block()

        with ExitStack() as es2:
            a = alloc_attn_tensors(nc, es2)
            mk.wait_stores("sync")
            mk.wait_stores("gpsimd")
            emit_attn(mk, a, xm1, xm2, WQt, SMAt, p1, p2)
            mk.run_block()

        with ExitStack() as es3:
            t2 = alloc_ffn_tensors(nc, es3, "f2")
            mk.wait_stores("sync")
            mk.wait_stores("gpsimd")
            emit_ffn_call(mk, t2, xm2, xo, W13t, W2t, gwt_s, eb_s, p1, p2)
            mk.wait_stores("sync")
            mk.run_block()
    return nc


def _prep_weights(inp):
    w1, w3, w2 = inp["L0_ffn_w1"], inp["L0_ffn_w3"], inp["L0_ffn_w2"]
    sw1, sw3, sw2 = inp["L0_ffn_sw1"], inp["L0_ffn_sw3"], inp["L0_ffn_sw2"]
    w1u = np.concatenate([w1, sw1.reshape(D, 2, TI).transpose(1, 0, 2)], 0)
    w3u = np.concatenate([w3, sw3.reshape(D, 2, TI).transpose(1, 0, 2)], 0)
    w2u = np.concatenate([w2, sw2.reshape(2, TI, D)], 0)
    W13 = np.ascontiguousarray(np.stack([
        w1u.reshape(NU, ND, 128, NI, 128).transpose(0, 3, 2, 1, 4),
        w3u.reshape(NU, ND, 128, NI, 128).transpose(0, 3, 2, 1, 4)], axis=1))
    W2 = np.ascontiguousarray(w2u.reshape(NU, NI, 128, 4, 512).transpose(0, 3, 2, 1, 4))
    GWT = np.ascontiguousarray(inp["L0_ffn_gate_w"].T)
    EB = np.ascontiguousarray(np.tile(inp["L0_ffn_expert_bias"][None, :], (128, 1)))
    WQ = np.ascontiguousarray(inp["L0_attn_W_Q"])
    cos, sin = _rope_tables()
    cosl = cos.reshape(NT, 128, 64).transpose(1, 0, 2).reshape(128, NT * 64)
    sinl = sin.reshape(NT, 128, 64).transpose(1, 0, 2).reshape(128, NT * 64)
    SMA = np.ascontiguousarray(np.concatenate([
        np.tile(inp["L0_attn_hh_b"][None, :], (128, 1)),
        inp["L0_attn_gn_weight"].reshape(16, 128).T,
        inp["L0_attn_gn_bias"].reshape(16, 128).T,
        cosl, sinl], axis=1)).astype(np.float32)
    CONSTS = np.tile(np.array([[EPS, -0.5]], np.float32), (128, 1))
    return W13, W2, GWT, EB, WQ, SMA, CONSTS


def kernel(**inp):
    _ = inp.get("L0_attn_S_n")
    X = inp["L0_attn_X"]
    W13, W2, GWT, EB, WQ, SMA, CONSTS = _prep_weights(inp)
    if "nc" not in _CACHE:
        _CACHE["nc"] = build_program()
    nc = _CACHE["nc"]
    in_maps = [dict(x0=np.ascontiguousarray(X[c]), w13=W13, w2=W2, gwt=GWT, eb=EB,
                    wqd=WQ, sma=SMA, consts=CONSTS) for c in range(NCORES)]
    res = run_bass_kernel_spmd(nc, in_maps, core_ids=list(range(NCORES)))
    return np.stack([np.asarray(res.results[c]["xo"], dtype=np.float32) for c in range(NCORES)], axis=0)
